# version 06
# baseline (speedup 1.0000x reference)
import contextlib
import math
import os

import numpy as np
import ml_dtypes

import concourse.bass as bass
import concourse.mybir as mybir
from concourse.bass_utils import run_bass_kernel_spmd

F32 = mybir.dt.float32
BF16 = mybir.dt.bfloat16
AF = mybir.ActivationFunctionType
ALU = mybir.AluOpType
AX = mybir.AxisListType

B, T, C = 4, 4096, 1024
TOK = 2048
NHP = 8
HD = 64
LAM_INIT = 0.8 - 0.6 * math.exp(-0.3 * 4)
RMS_EPS = float(np.finfo(np.float32).eps)
LN_EPS = 1e-5
NEXP, FF = 8, 4096
FG = 1024
NGRP = FF // FG
TBS = 256
NTB = TOK // TBS
JIT2 = 0.02
NEG_BIG = -1e30

ENGS = ("sync", "gpsimd", "tensor", "vector", "scalar")


class Prog:

    def __init__(self, blk, sems):
        self.blk = blk
        self.sems = sems
        self.cnt = {k: 0 for k in sems}
        self.waited = {}
        self.q = {e: [] for e in ENGS}

    SELF_SEM = {"vector": "V", "scalar": "S"}

    def op(self, eng, fn, waits=(), inc=None, inc_n=1):
        waits = [(s, v) for (s, v) in waits if v and v > 0]
        own = self.SELF_SEM.get(eng)
        if own is not None and fn is not None:
            if self.cnt[own] > 0:
                waits.append((own, self.cnt[own]))
            if inc is None:
                inc = own
        if inc is not None:
            self.cnt[inc] += inc_n
        self.q[eng].append((waits, fn, inc, inc_n))
        return self.cnt[inc] if inc is not None else None

    def barrier(self):
        vals = [(k, v) for k, v in self.cnt.items() if v > 0]
        for e in ENGS:
            self.op(e, None, waits=vals)

    def flush(self):
        for e in ENGS:
            ops = self.q[e]
            if not ops:
                continue
            sec = getattr(self.blk, e)

            def body(h, eng_name=e, ops_local=list(ops)):
                for waits, fn, inc, inc_n in ops_local:
                    for s, v in waits:
                        key = (eng_name, s)
                        if self.waited.get(key, 0) < v:
                            h.wait_ge(self.sems[s], v)
                            self.waited[key] = v
                    if fn is not None:
                        instr = fn(h)
                        if inc is not None:
                            instr.then_inc(self.sems[inc], inc_n)

            sec(body)
            self.q[e] = []


def build_program(debug=False, phases=5):
    nc = bass.Bass("TRN2", target_bir_lowering=False)

    d_x_own = nc.dram_tensor("x_own", [TOK, C], F32, kind="ExternalInput").ap()
    d_x_ctx = nc.dram_tensor("x_ctx", [T, C], F32, kind="ExternalInput").ap()
    d_wq = nc.dram_tensor("wq_t", [C, C], F32, kind="ExternalInput").ap()
    d_wk = nc.dram_tensor("wk_t", [C, C], F32, kind="ExternalInput").ap()
    d_wv = nc.dram_tensor("wv_t", [C, C], F32, kind="ExternalInput").ap()
    d_wp = nc.dram_tensor("wproj_t", [C, C], F32, kind="ExternalInput").ap()
    d_gate = nc.dram_tensor("gate_t", [C, NEXP], F32, kind="ExternalInput").ap()
    d_w1 = nc.dram_tensor("w1_t", [NEXP, C, FF], F32, kind="ExternalInput").ap()
    d_w3 = nc.dram_tensor("w3_t", [NEXP, C, FF], F32, kind="ExternalInput").ap()
    d_w2 = nc.dram_tensor("w2_t", [NEXP, FF, C], F32, kind="ExternalInput").ap()
    d_lqk = nc.dram_tensor("lqk", [1, 4 * HD], F32, kind="ExternalInput").ap()
    d_cos_o = nc.dram_tensor("cos_own", [TOK, 512], F32, kind="ExternalInput").ap()
    d_sin_o = nc.dram_tensor("sin_own", [TOK, 512], F32, kind="ExternalInput").ap()
    d_cos_c = nc.dram_tensor("cos_ctx", [T, 512], F32, kind="ExternalInput").ap()
    d_sin_c = nc.dram_tensor("sin_ctx", [T, 512], F32, kind="ExternalInput").ap()
    d_mask = nc.dram_tensor("mask", [128, 1024], BF16, kind="ExternalInput").ap()
    d_ident = nc.dram_tensor("ident", [128, 128], BF16, kind="ExternalInput").ap()

    d_out = nc.dram_tensor("out", [TOK, C], F32, kind="ExternalOutput").ap()

    d_xn_o = nc.dram_tensor("xn_own_s", [TOK, C], BF16, kind="Internal").ap()
    d_xn_c = nc.dram_tensor("xn_ctx_s", [T, C], BF16, kind="Internal").ap()
    d_q_s = nc.dram_tensor("q_s", [TOK, C], BF16, kind="Internal").ap()
    d_k_s = nc.dram_tensor("k_s", [T, C], BF16, kind="Internal").ap()
    d_v_s = nc.dram_tensor("v_s", [T, C], BF16, kind="Internal").ap()
    d_avn_s = nc.dram_tensor("avn_s", [TOK, C], BF16, kind="Internal").ap()

    dbg = {}
    if debug:
        for name, shp, dt in [
            ("dbg_xn_own", [TOK, C], BF16), ("dbg_xn_ctx", [T, C], BF16),
            ("dbg_q", [TOK, C], BF16), ("dbg_k", [T, C], BF16), ("dbg_v", [T, C], BF16),
            ("dbg_wtok", [128, 128], F32), ("dbg_y", [128, 16384], F32),
            ("dbg_avn", [TOK, C], BF16),
        ]:
            dbg[name] = nc.dram_tensor(name, shp, dt, kind="ExternalOutput").ap()

    with contextlib.ExitStack() as root:
        E = root.enter_context
        semnames = ("T", "V", "S", "dG", "dG0", "dG1", "dT",
                    "dX0", "dX1", "dX2", "dW0", "dW1",
                    "dTab0", "dTab1", "dQ0", "dQ1",
                    "dA0", "dA1", "dN0", "dN1", "dO0", "dO1")
        sems = {k: E(nc.semaphore(k)) for k in semnames}
        blk = E(nc.Block())
        P = Prog(blk, sems)

        wtok_sb = E(nc.sbuf_tensor("wtok_sb", [128, 128], F32))
        lamneg = E(nc.sbuf_tensor("lamneg", [128, 1], F32))
        y_sb = E(nc.sbuf_tensor("y_sb", [128, 16 * 1024], F32))
        identb = E(nc.sbuf_tensor("identb", [128, 128], BF16))
        constb = E(nc.sbuf_tensor("constb", [128, 4], F32))

        g_id = P.op("gpsimd", lambda h: h.dma_start(identb[:, :], d_ident),
                    inc="dG", inc_n=16)
        P.op("vector", lambda h: h.memset(constb[:, 0:1], LN_EPS))
        P.op("vector", lambda h: h.memset(constb[:, 1:2], RMS_EPS))
        P.op("vector", lambda h: h.memset(constb[:, 2:3], 1.0 / HD))
        P.op("vector", lambda h: h.memset(constb[:, 3:4], -0.5), inc="V")

        with contextlib.ExitStack() as ph:
          if phases >= 1:
              PE = ph.enter_context
              xb = [PE(nc.sbuf_tensor(f"p0_x{j}", [128, C], F32)) for j in range(3)]
              sqs = [PE(nc.sbuf_tensor(f"p0_sq{j}", [128, C], F32)) for j in range(2)]
              xnb = [PE(nc.sbuf_tensor(f"p0_xn{j}", [128, C], BF16)) for j in range(2)]
              st = [PE(nc.sbuf_tensor(f"p0_st{j}", [128, 8], F32)) for j in range(2)]

              tiles = [("own", t) for t in range(16)] + [("ctx", t) for t in range(32)]
              s_xn, sp_done, v_fin, ssq_d = {}, {}, {}, {}
              for idx, (kind, t) in enumerate(tiles):
                  src = d_x_own if kind == "own" else d_x_ctx
                  dst = d_xn_o if kind == "own" else d_xn_c
                  b3, b2 = idx % 3, idx % 2
                  x_t, sq_t, xn_t, s_t = xb[b3], sqs[b2], xnb[b2], st[b2]
                  w_guard = []
                  if idx - 3 in s_xn:
                      w_guard = [("V", s_xn[idx - 3]), ("S", ssq_d[idx - 3])]
                  xsem = f"dX{idx % 3}"
                  ld = P.op("sync", lambda h, a=x_t, s=src, t=t: h.dma_start(
                      a[:, :], s[t * 128:(t + 1) * 128, :]), waits=w_guard, inc=xsem, inc_n=16)
                  g2 = [("V", v_fin[idx - 2])] if idx - 2 in v_fin else []
                  P.op("vector", lambda h, a=s_t, x=x_t: h.reduce_sum(
                      a[:, 0:1], x[:, :], axis=AX.X), waits=[(xsem, ld)] + g2)
                  ssq = P.op("scalar", lambda h, a=sq_t, x=x_t, s=s_t: h.activation(
                      a[:, :], x[:, :], AF.Square, accum_out=s[:, 1:2]),
                      waits=[(xsem, ld)] + g2, inc="S")
                  ssq_d[idx] = ssq
                  P.op("vector", lambda h, s=s_t: h.tensor_scalar(
                      s[:, 2:3], s[:, 0:1], 1.0 / C, None, ALU.mult))
                  P.op("vector", lambda h, s=s_t: h.tensor_scalar(
                      s[:, 3:4], s[:, 1:2], 1.0 / C, None, ALU.mult), waits=[("S", ssq)])
                  P.op("vector", lambda h, s=s_t: h.tensor_scalar(
                      s[:, 4:5], s[:, 2:3], s[:, 2:3], None, ALU.mult))
                  vv = P.op("vector", lambda h, s=s_t: h.tensor_sub(
                      s[:, 5:6], s[:, 3:4], s[:, 4:5]), inc="V")
                  P.op("scalar", lambda h, s=s_t: h.activation(
                      s[:, 6:7], s[:, 5:6], AF.Ln, bias=constb[:, 0:1]),
                      waits=[("V", vv)], inc="S")
                  sd = P.op("scalar", lambda h, s=s_t: h.activation(
                      s[:, 6:7], s[:, 6:7], AF.Exp, scale=constb[:, 3:4]), inc="S")
                  P.op("vector", lambda h, s=s_t: h.tensor_scalar(
                      s[:, 7:8], s[:, 6:7], -1.0, None, ALU.mult), waits=[("S", sd)])
                  vn = P.op("vector", lambda h, s=s_t: h.tensor_scalar(
                      s[:, 7:8], s[:, 2:3], s[:, 7:8], None, ALU.mult), inc="V")
                  v_fin[idx] = vn
                  wsem = f"dW{idx % 2}"
                  xg = [(wsem, sp_done[idx - 2])] if idx - 2 in sp_done else []
                  sx = P.op("vector", lambda h, a=xn_t, x=x_t, s=s_t: h.tensor_scalar(
                      a[:, :], x[:, :], s[:, 6:7], s[:, 7:8], ALU.mult, ALU.add),
                      waits=xg, inc="V")
                  s_xn[idx] = sx
                  sp_done[idx] = P.op("sync", lambda h, a=xn_t, d=dst, t=t: h.dma_start(
                      d[t * 128:(t + 1) * 128, :], a[:, :]), waits=[("V", sx)],
                      inc=wsem, inc_n=16)
              P.barrier()
              P.flush()

        with contextlib.ExitStack() as p1:
          if phases >= 2:
              P1E = p1.enter_context
              qps = [P1E(nc.psum_tensor(f"qps{j}", [128, 512], F32)) for j in range(2)]
              ctab = [P1E(nc.sbuf_tensor(f"ctab{j}", [128, 512], F32)) for j in range(2)]
              stab = [P1E(nc.sbuf_tensor(f"stab{j}", [128, 512], F32)) for j in range(2)]
              sq5 = [P1E(nc.sbuf_tensor(f"sq5{j}", [128, 512], F32)) for j in range(2)]
              qn5 = [P1E(nc.sbuf_tensor(f"qn5{j}", [128, 512], F32)) for j in range(2)]
              ab5 = [P1E(nc.sbuf_tensor(f"ab5{j}", [128, 512], F32)) for j in range(2)]
              bb5 = [P1E(nc.sbuf_tensor(f"bb5{j}", [128, 512], F32)) for j in range(2)]
              s8 = [P1E(nc.sbuf_tensor(f"s8_{j}", [128, 16], F32)) for j in range(2)]
              qrb = [P1E(nc.sbuf_tensor(f"qrb{j}", [128, 512], BF16)) for j in range(2)]

              s_post, v_post, sp_q = {}, {}, {}

              def qkv_part(tag, xnT, n_m, wsb, gw, tlw, dsp, dcos, dsin, has_rope, is_q):
                  v_tab = {}
                  for m in range(n_m):
                      ct = None
                      tsem = f"dTab{m % 2}"
                      if has_rope:
                          tg2 = [("V", v_tab[m - 2])] if m - 2 in v_tab else []
                          P.op("sync", lambda h, m=m, dcos=dcos: h.dma_start(
                              ctab[m % 2][:, :], dcos[m * 128:(m + 1) * 128, :]),
                              waits=tg2, inc=tsem, inc_n=16)
                          ct = P.op("sync", lambda h, m=m, dsin=dsin: h.dma_start(
                              stab[m % 2][:, :], dsin[m * 128:(m + 1) * 128, :]),
                              inc=tsem, inc_n=16)
                      for n in range(2):
                          bi = (m * 2 + n) % 2
                          pq = qps[bi]
                          guard = []
                          if bi in s_post:
                              guard.append(("S", s_post[bi]))
                          if bi in v_post:
                              guard.append(("V", v_post[bi]))
                          tq = None
                          for k in range(8):
                              r = P.op("tensor", lambda h, k=k, m=m, n=n, pq=pq, xnT=xnT, wsb=wsb: h.matmul(
                                  pq[:, :], xnT[k][:, m * 128:(m + 1) * 128],
                                  wsb[:, k * C + n * 512: k * C + n * 512 + 512],
                                  start=(k == 0), stop=(k == 7)),
                                  waits=([("dG", gw), ("dT", tlw)] + guard) if k == 0 else [],
                                  inc="T" if k == 7 else None)
                              if r is not None:
                                  tq = r
                          if has_rope:
                              b2 = bi
                              ssq = P.op("scalar", lambda h, pq=pq, b2=b2: h.activation(
                                  sq5[b2][:, :], pq[:, :], AF.Square), waits=[("T", tq)], inc="S")
                              vred = None
                              for j in range(8):
                                  vred = P.op("vector", lambda h, j=j, b2=b2: h.reduce_sum(
                                      s8[b2][:, j:j + 1], sq5[b2][:, j * 64:(j + 1) * 64],
                                      axis=AX.X), waits=[("S", ssq)] if j == 0 else [],
                                      inc="V" if j == 7 else None)
                              P.op("scalar", lambda h, b2=b2: h.activation(
                                  s8[b2][:, 8:16], s8[b2][:, 0:8], AF.Ln,
                                  scale=constb[:, 2:3], bias=constb[:, 1:2]),
                                  waits=[("V", vred)], inc="S")
                              sd8 = P.op("scalar", lambda h, b2=b2: h.activation(
                                  s8[b2][:, 8:16], s8[b2][:, 8:16], AF.Exp,
                                  scale=constb[:, 3:4]), inc="S")
                              if is_q:
                                  P.op("vector", lambda h, b2=b2: h.tensor_scalar(
                                      s8[b2][:, 8:16], s8[b2][:, 8:16], 0.125, None, ALU.mult),
                                      waits=[("S", sd8)])
                              vpost = None
                              for j in range(8):
                                  vpost = P.op("vector", lambda h, j=j, pq=pq, b2=b2: h.tensor_scalar(
                                      qn5[b2][:, j * 64:(j + 1) * 64], pq[:, j * 64:(j + 1) * 64],
                                      s8[b2][:, 8 + j:9 + j], None, ALU.mult),
                                      waits=[("S", sd8)] if j == 0 else [],
                                      inc="V" if j == 7 else None)
                              v_post[b2] = vpost
                              s_post[b2] = ssq
                              P.op("vector", lambda h, b2=b2, m=m: h.tensor_mul(
                                  ab5[b2][:, :], qn5[b2][:, :], ctab[m % 2][:, :]),
                                  waits=[(tsem, ct)])
                              vr = P.op("vector", lambda h, b2=b2, m=m: h.tensor_mul(
                                  bb5[b2][:, :], qn5[b2][:, :], stab[m % 2][:, :]), inc="V")
                              v_tab[m] = vr
                              spg = [(f"dQ{b2}", sp_q[b2])] if b2 in sp_q else []
                              P.op("vector", lambda h, b2=b2: h.tensor_add(
                                  qrb[b2][:, :].rearrange("p (g d) -> p g d", d=64)[:, :, 0:32],
                                  ab5[b2][:, :].rearrange("p (g d) -> p g d", d=64)[:, :, 0:32],
                                  bb5[b2][:, :].rearrange("p (g d) -> p g d", d=64)[:, :, 32:64]),
                                  waits=spg)
                              vr2 = P.op("vector", lambda h, b2=b2: h.tensor_sub(
                                  qrb[b2][:, :].rearrange("p (g d) -> p g d", d=64)[:, :, 32:64],
                                  ab5[b2][:, :].rearrange("p (g d) -> p g d", d=64)[:, :, 32:64],
                                  bb5[b2][:, :].rearrange("p (g d) -> p g d", d=64)[:, :, 0:32]),
                                  inc="V")
                              sp_q[b2] = P.op("sync", lambda h, b2=b2, m=m, n=n, dsp=dsp: h.dma_start(
                                  dsp[m * 128:(m + 1) * 128, n * 512:n * 512 + 512],
                                  qrb[b2][:, :]), waits=[("V", vr2)], inc=f"dQ{b2}", inc_n=16)
                          else:
                              spg = [(f"dQ{bi}", sp_q[bi])] if bi in sp_q else []
                              sv = P.op("scalar", lambda h, pq=pq, bi=bi: h.activation(
                                  qrb[bi][:, :], pq[:, :], AF.Copy),
                                  waits=[("T", tq)] + spg, inc="S")
                              s_post[bi] = sv
                              v_post.pop(bi, None)
                              sp_q[bi] = P.op("sync", lambda h, bi=bi, m=m, n=n, dsp=dsp: h.dma_start(
                                  dsp[m * 128:(m + 1) * 128, n * 512:n * 512 + 512],
                                  qrb[bi][:, :]), waits=[("S", sv)], inc=f"dQ{bi}", inc_n=16)

              with contextlib.ExitStack() as pq_scope:
                  QE = pq_scope.enter_context
                  xnT_o = [QE(nc.sbuf_tensor(f"xnT_o{k}", [128, TOK], BF16,
                                             is_transpose_dest=True)) for k in range(8)]
                  wqb = QE(nc.sbuf_tensor("wqb", [128, 8 * C], BF16))
                  gateb = QE(nc.sbuf_tensor("gateb", [128, 64], BF16))
                  lqkb = QE(nc.sbuf_tensor("lqkb", [128, 4 * HD], F32))
                  gps = [QE(nc.psum_tensor(f"gps{j}", [128, 512], F32)) for j in range(2)]
                  rt = QE(nc.sbuf_tensor("rt", [128, 8], F32))
                  r8 = QE(nc.sbuf_tensor("r8", [128, 72], F32))
                  lt = QE(nc.sbuf_tensor("lt", [128, HD], F32))
                  lam_c = QE(nc.sbuf_tensor("lam_c", [128, 2], F32))

                  tl_o = None
                  for k in range(8):
                      tl_o = P.op("sync", lambda h, k=k: h.dma_start_transpose(
                          xnT_o[k][:, :], d_xn_o[:, k * 128:(k + 1) * 128]),
                          inc="dT", inc_n=16)
                  g_wq = None
                  for k in range(8):
                      g_wq = P.op("gpsimd", lambda h, k=k: h.dma_start(
                          wqb[:, k * C:(k + 1) * C], d_wq[k * 128:(k + 1) * 128, :]),
                          inc="dG", inc_n=16)
                  g_gate = P.op("gpsimd", lambda h: h.dma_start(
                      gateb[:, :].rearrange("p (k e) -> p k e", e=NEXP),
                      d_gate.rearrange("(k p) e -> p k e", p=128)), inc="dG", inc_n=16)
                  g_lqk = P.op("gpsimd", lambda h: h.dma_start(
                      lqkb[:, :], d_lqk.broadcast_to([128, 4 * HD])), inc="dG", inc_n=16)
                  g_wq = g_gate = g_lqk

                  P.op("vector", lambda h: h.tensor_mul(
                      lt[:, :], lqkb[:, 0:64], lqkb[:, 64:128]), waits=[("dG", g_lqk)])
                  P.op("vector", lambda h: h.reduce_sum(lam_c[:, 0:1], lt[:, :], axis=AX.X))
                  P.op("vector", lambda h: h.tensor_mul(
                      lt[:, :], lqkb[:, 128:192], lqkb[:, 192:256]))
                  vl = P.op("vector", lambda h: h.reduce_sum(
                      lam_c[:, 1:2], lt[:, :], axis=AX.X), inc="V")
                  P.op("scalar", lambda h: h.activation(
                      lam_c[:, 0:1], lam_c[:, 0:1], AF.Exp), waits=[("V", vl)], inc="S")
                  sl = P.op("scalar", lambda h: h.activation(
                      lam_c[:, 1:2], lam_c[:, 1:2], AF.Exp), inc="S")
                  P.op("vector", lambda h: h.tensor_sub(
                      lamneg[:, 0:1], lam_c[:, 0:1], lam_c[:, 1:2]), waits=[("S", sl)])
                  P.op("vector", lambda h: h.tensor_scalar(
                      lamneg[:, 0:1], lamneg[:, 0:1], LAM_INIT, -1.0, ALU.add, ALU.mult))

                  v_wtok, s_e2g = {}, {}
                  for m in range(16):
                      pg = gps[m % 2]
                      guard = []
                      if m - 2 in v_wtok:
                          guard = [("V", v_wtok[m - 2]), ("S", s_e2g[m - 2])]
                      tg = None
                      for k in range(8):
                          r = P.op("tensor", lambda h, k=k, m=m, pg=pg: h.matmul(
                              pg[:, 0:8], xnT_o[k][:, m * 128:(m + 1) * 128],
                              gateb[:, k * 8:(k + 1) * 8], start=(k == 0), stop=(k == 7)),
                              waits=([("dT", tl_o), ("dG", g_gate)] + guard) if k == 0 else [],
                              inc="T" if k == 7 else None)
                          if r is not None:
                              tg = r
                      ag = [("V", v_wtok[m - 1])] if m - 1 in v_wtok else []
                      P.op("vector", lambda h, pg=pg: h.reduce_max(
                          rt[:, 0:1], pg[:, 0:8], axis=AX.X), waits=[("T", tg)])
                      P.op("vector", lambda h, pg=pg: h.tensor_scalar(
                          r8[:, 0:8], pg[:, 0:8], rt[:, 0:1], None, ALU.is_equal))
                      sab = P.op("scalar", lambda h, pg=pg: h.activation(
                          r8[:, 8:16], pg[:, 0:8], AF.Abs), waits=[("T", tg)] + ag, inc="S")
                      P.op("vector", lambda h: h.tensor_scalar(
                          r8[:, 16:24], r8[:, 8:16], rt[:, 0:1], None, ALU.max),
                          waits=[("S", sab)])
                      P.op("vector", lambda h, pg=pg: h.tensor_scalar(
                          r8[:, 24:32], pg[:, 0:8], rt[:, 0:1], None, ALU.subtract))
                      P.op("vector", lambda h: h.scalar_tensor_tensor(
                          r8[:, 32:40], r8[:, 16:24], JIT2, r8[:, 24:32], ALU.mult, ALU.add))
                      P.op("vector", lambda h: h.tensor_scalar(
                          r8[:, 32:40], r8[:, 32:40], 0.0, None, ALU.is_ge))
                      vm = P.op("vector", lambda h: h.tensor_scalar(
                          rt[:, 1:2], rt[:, 0:1], -1.0, None, ALU.mult), inc="V")
                      se = P.op("scalar", lambda h, pg=pg: h.activation(
                          r8[:, 40:48], pg[:, 0:8], AF.Exp, bias=rt[:, 1:2]),
                          waits=[("V", vm)], inc="S")
                      P.op("vector", lambda h: h.tensor_mul(
                          r8[:, 40:48], r8[:, 40:48], r8[:, 32:40]), waits=[("S", se)])
                      P.op("vector", lambda h: h.reduce_sum(rt[:, 4:5], r8[:, 40:48], axis=AX.X))
                      P.op("vector", lambda h: h.reciprocal(rt[:, 6:7], rt[:, 4:5]))
                      P.op("vector", lambda h, pg=pg: h.scalar_tensor_tensor(
                          r8[:, 56:64], r8[:, 0:8], NEG_BIG, pg[:, 0:8], ALU.mult, ALU.add))
                      P.op("vector", lambda h: h.reduce_max(rt[:, 2:3], r8[:, 56:64], axis=AX.X))
                      P.op("vector", lambda h: h.tensor_scalar(
                          r8[:, 48:56], r8[:, 56:64], rt[:, 2:3], None, ALU.is_equal))
                      P.op("vector", lambda h: h.tensor_scalar(
                          r8[:, 16:24], r8[:, 8:16], rt[:, 2:3], None, ALU.max))
                      P.op("vector", lambda h, pg=pg: h.tensor_scalar(
                          r8[:, 24:32], pg[:, 0:8], rt[:, 2:3], None, ALU.subtract))
                      P.op("vector", lambda h: h.scalar_tensor_tensor(
                          r8[:, 24:32], r8[:, 16:24], JIT2, r8[:, 24:32], ALU.mult, ALU.add))
                      P.op("vector", lambda h: h.tensor_scalar(
                          r8[:, 24:32], r8[:, 24:32], 0.0, None, ALU.is_ge))
                      P.op("vector", lambda h: h.tensor_sub(
                          r8[:, 24:32], r8[:, 24:32], r8[:, 0:8]))
                      vm2 = P.op("vector", lambda h: h.tensor_scalar(
                          rt[:, 3:4], rt[:, 2:3], -1.0, None, ALU.mult), inc="V")
                      se2 = P.op("scalar", lambda h, pg=pg: h.activation(
                          r8[:, 64:72], pg[:, 0:8], AF.Exp, bias=rt[:, 3:4]),
                          waits=[("V", vm2)], inc="S")
                      s_e2g[m] = se2
                      P.op("vector", lambda h: h.tensor_mul(
                          r8[:, 64:72], r8[:, 64:72], r8[:, 24:32]), waits=[("S", se2)])
                      P.op("vector", lambda h: h.reduce_sum(rt[:, 5:6], r8[:, 64:72], axis=AX.X))
                      P.op("vector", lambda h: h.reciprocal(rt[:, 7:8], rt[:, 5:6]))
                      P.op("vector", lambda h: h.tensor_scalar(
                          r8[:, 48:56], r8[:, 48:56], rt[:, 7:8], None, ALU.mult))
                      v_wtok[m] = P.op("vector", lambda h, m=m: h.scalar_tensor_tensor(
                          wtok_sb[:, m * 8:(m + 1) * 8], r8[:, 0:8], rt[:, 6:7],
                          r8[:, 48:56], ALU.mult, ALU.add), inc="V")

                  qkv_part("q", xnT_o, 16, wqb, g_wq, tl_o, d_q_s, d_cos_o, d_sin_o, True, True)
                  P.barrier()
                  P.flush()

              with contextlib.ExitStack() as pk_scope:
                  KE = pk_scope.enter_context
                  xnT_c = [KE(nc.sbuf_tensor(f"xnT_c{k}", [128, T], BF16,
                                             is_transpose_dest=True)) for k in range(8)]
                  wkb = KE(nc.sbuf_tensor("wkb", [128, 8 * C], BF16))
                  wvb = KE(nc.sbuf_tensor("wvb", [128, 8 * C], BF16))
                  tl_c = None
                  for k in range(8):
                      tl_c = P.op("sync", lambda h, k=k: h.dma_start_transpose(
                          xnT_c[k][:, :], d_xn_c[:, k * 128:(k + 1) * 128]),
                          inc="dT", inc_n=16)
                  g_wk = None
                  for k in range(8):
                      g_wk = P.op("gpsimd", lambda h, k=k: h.dma_start(
                          wkb[:, k * C:(k + 1) * C], d_wk[k * 128:(k + 1) * 128, :]),
                          inc="dG", inc_n=16)
                  g_wv = None
                  for k in range(8):
                      g_wv = P.op("gpsimd", lambda h, k=k: h.dma_start(
                          wvb[:, k * C:(k + 1) * C], d_wv[k * 128:(k + 1) * 128, :]),
                          inc="dG", inc_n=16)
                  g_wk = g_wv
                  qkv_part("k", xnT_c, 32, wkb, g_wk, tl_c, d_k_s, d_cos_c, d_sin_c, True, False)
                  qkv_part("v", xnT_c, 32, wvb, g_wv, tl_c, d_v_s, None, None, False, False)
                  P.barrier()
                  P.flush()

        with contextlib.ExitStack() as ph:
          if phases >= 3:
              PE = ph.enter_context
              xnT_o = [PE(nc.sbuf_tensor(f"f_xnT{k}", [128, TOK], BF16,
                                         is_transpose_dest=True)) for k in range(8)]
              w1c = [PE(nc.sbuf_tensor(f"w1c{j}", [128, 8192], BF16)) for j in range(2)]
              w3c = [PE(nc.sbuf_tensor(f"w3c{j}", [128, 8192], BF16)) for j in range(2)]
              w2c = [PE(nc.sbuf_tensor(f"w2c{j}", [128, 8192], BF16)) for j in range(2)]
              sbuf_s = [PE(nc.sbuf_tensor(f"f_s{j}", [128, TBS], BF16)) for j in range(2)]
              hbuf = [PE(nc.sbuf_tensor(f"f_h{j}", [128, TBS], BF16)) for j in range(2)]
              ups = [PE(nc.psum_tensor(f"ups{j}", [128, 512], F32)) for j in range(2)]
              wps = [PE(nc.psum_tensor(f"wps{j}", [128, 512], F32)) for j in range(2)]
              yps = PE(nc.psum_tensor("yps", [128, 2048], F32))

              tl = None
              for k in range(8):
                  tl = P.op("sync", lambda h, k=k: h.dma_start_transpose(
                      xnT_o[k][:, :], d_xn_o[:, k * 128:(k + 1) * 128]), inc="dT", inc_n=16)
              P.op("vector", lambda h: h.memset(y_sb[:, :], 0.0), inc="V")

              groups = [(e, fg) for e in range(NEXP) for fg in range(NGRP)]
              pe_end, dg_end, v_ystt, t_mm2 = {}, {}, {}, {}
              u_guard, w_guard, s_guard, h_guard = {}, {}, {}, {}
              for g, (e, fg) in enumerate(groups):
                  buf = g % 2
                  gg = [("T", pe_end[g - 2])] if g - 2 in pe_end else []
                  gsem = f"dG{g % 2}"
                  for k in range(8):
                      P.op("gpsimd", lambda h, e=e, fg=fg, k=k, buf=buf: h.dma_start(
                          w1c[buf][:, k * 1024:(k + 1) * 1024],
                          d_w1[e, k * 128:(k + 1) * 128, fg * FG:(fg + 1) * FG]),
                          waits=gg if k == 0 else [], inc=gsem, inc_n=16)
                  for k in range(8):
                      P.op("gpsimd", lambda h, e=e, fg=fg, k=k, buf=buf: h.dma_start(
                          w3c[buf][:, k * 1024:(k + 1) * 1024],
                          d_w3[e, k * 128:(k + 1) * 128, fg * FG:(fg + 1) * FG]),
                          inc=gsem, inc_n=16)
                  for ft in range(8):
                      dg_end[g] = P.op("gpsimd", lambda h, e=e, fg=fg, ft=ft, buf=buf: h.dma_start(
                          w2c[buf][:, ft * 1024:(ft + 1) * 1024],
                          d_w2[e, fg * FG + ft * 128: fg * FG + (ft + 1) * 128, :]),
                          inc=gsem, inc_n=16)

                  for tb in range(NTB):
                      t_mm1, t_mm3, s_silu, v_h = {}, {}, {}, {}

                      def emit_mm2(fm, g=g, tb=tb, buf=buf, v_h=None):
                          sl = fm % 2
                          w = [("V", v_h[fm])]
                          if fm == 0:
                              if (g, tb - 1) in v_ystt:
                                  w.append(("V", v_ystt[(g, tb - 1)]))
                              elif (g - 1, NTB - 1) in v_ystt:
                                  w.append(("V", v_ystt[(g - 1, NTB - 1)]))
                          last = None
                          for half in range(2):
                              for n2 in range(2):
                                  last = P.op("tensor",
                                              lambda h, sl=sl, half=half, n2=n2, fm=fm, buf=buf: h.matmul(
                                      yps[:, half * 1024 + n2 * 512: half * 1024 + n2 * 512 + 512],
                                      hbuf[sl][:, half * 128:(half + 1) * 128],
                                      w2c[buf][:, fm * 1024 + n2 * 512: fm * 1024 + n2 * 512 + 512],
                                      start=(fm == 0), stop=(fm == 7)),
                                      waits=w if (half == 0 and n2 == 0) else [],
                                      inc="T")
                          t_mm2[(g, tb, fm)] = last
                          h_guard[sl] = ("T", last)

                      for fm in range(8):
                          sl = fm % 2
                          w1l = [(f"dG{g % 2}", dg_end[g])] if (tb == 0 and fm == 0) else []
                          if sl in u_guard:
                              w1l.append(u_guard[sl])
                          if tb == 0 and fm == 0 and g == 0:
                              w1l.append(("dT", tl))
                          tm1 = None
                          for k in range(8):
                              r = P.op("tensor",
                                       lambda h, k=k, fm=fm, tb=tb, buf=buf, sl=sl: h.matmul(
                                  ups[sl][:, 0:TBS],
                                  w1c[buf][:, k * 1024 + fm * 128: k * 1024 + (fm + 1) * 128],
                                  xnT_o[k][:, tb * TBS:(tb + 1) * TBS],
                                  start=(k == 0), stop=(k == 7)),
                                  waits=w1l if k == 0 else [], inc="T" if k == 7 else None)
                              if r is not None:
                                  tm1 = r
                          t_mm1[fm] = tm1
                          w3l = [w_guard[sl]] if sl in w_guard else []
                          tm3 = None
                          for k in range(8):
                              r = P.op("tensor",
                                       lambda h, k=k, fm=fm, tb=tb, buf=buf, sl=sl: h.matmul(
                                  wps[sl][:, 0:TBS],
                                  w3c[buf][:, k * 1024 + fm * 128: k * 1024 + (fm + 1) * 128],
                                  xnT_o[k][:, tb * TBS:(tb + 1) * TBS],
                                  start=(k == 0), stop=(k == 7)),
                                  waits=w3l if k == 0 else [], inc="T" if k == 7 else None)
                              if r is not None:
                                  tm3 = r
                          t_mm3[fm] = tm3
                          if fm > 0:
                              emit_mm2(fm - 1, v_h=v_h)
                          sg = [s_guard[sl]] if sl in s_guard else []
                          s_silu[fm] = P.op("scalar", lambda h, sl=sl: h.activation(
                              sbuf_s[sl][:, :], ups[sl][:, 0:TBS], AF.Silu),
                              waits=[("T", t_mm1[fm])] + sg, inc="S")
                          u_guard[sl] = ("S", s_silu[fm])
                          hg = [h_guard[sl]] if sl in h_guard else []
                          v_h[fm] = P.op("vector", lambda h, sl=sl: h.tensor_mul(
                              hbuf[sl][:, :], sbuf_s[sl][:, :], wps[sl][:, 0:TBS]),
                              waits=[("S", s_silu[fm]), ("T", t_mm3[fm])] + hg, inc="V")
                          w_guard[sl] = ("V", v_h[fm])
                          s_guard[sl] = ("V", v_h[fm])
                      emit_mm2(7, v_h=v_h)
                      tile0 = tb * 2
                      for half in range(2):
                          v_ystt[(g, tb)] = P.op("vector",
                                                 lambda h, half=half, tile0=tile0, e=e: h.scalar_tensor_tensor(
                              y_sb[:, (tile0 + half) * 1024:(tile0 + half + 1) * 1024],
                              yps[:, half * 1024:(half + 1) * 1024],
                              wtok_sb[:, (tile0 + half) * 8 + e: (tile0 + half) * 8 + e + 1],
                              y_sb[:, (tile0 + half) * 1024:(tile0 + half + 1) * 1024],
                              ALU.mult, ALU.add),
                              waits=[("T", t_mm2[(g, tb, 7)])], inc="V")
                  pe_end[g] = t_mm2[(g, NTB - 1, 7)]
              P.barrier()
              P.flush()

        with contextlib.ExitStack() as ph:
          if phases >= 4:
              PE = ph.enter_context
              qt = [PE(nc.sbuf_tensor(f"a_qt{j}", [128, TOK], BF16, is_transpose_dest=True))
                    for j in range(2)]
              kt = [PE(nc.sbuf_tensor(f"a_kt{j}", [128, T], BF16, is_transpose_dest=True))
                    for j in range(2)]
              vh = [PE(nc.sbuf_tensor(f"a_vh{j}", [128, T], BF16)) for j in range(2)]
              e0b = [PE(nc.sbuf_tensor(f"a_e0{j}", [128, T], BF16)) for j in range(2)]
              e1b = [PE(nc.sbuf_tensor(f"a_e1{j}", [128, T], BF16)) for j in range(2)]
              ptrb = PE(nc.sbuf_tensor("a_ptr", [128, 4096], BF16))
              eraw = [PE(nc.sbuf_tensor(f"a_eraw{j}", [128, 512], BF16)) for j in range(2)]
              maskb = PE(nc.sbuf_tensor("a_mask", [128, 1024], BF16))
              zc = PE(nc.sbuf_tensor("a_zc", [128, 48], F32))
              avsq = PE(nc.sbuf_tensor("a_avsq", [128, 128], F32))
              avsb = [PE(nc.sbuf_tensor(f"a_avsb{j}", [128, 128], F32)) for j in range(2)]
              avnb = [PE(nc.sbuf_tensor(f"a_avn{j}", [128, 128], BF16)) for j in range(2)]
              s0ps = [PE(nc.psum_tensor(f"s0ps{j}", [128, 512], F32)) for j in range(2)]
              s1ps = [PE(nc.psum_tensor(f"s1ps{j}", [128, 512], F32)) for j in range(2)]
              ptrps = [PE(nc.psum_tensor(f"ptrps{j}", [128, 1024], BF16)) for j in range(2)]
              avps = [PE(nc.psum_tensor(f"avps{j}", [128, 512], F32)) for j in range(2)]

              gm = P.op("gpsimd", lambda h: h.dma_start(maskb[:, :], d_mask),
                        inc="dG", inc_n=16)
              t_avend = {}
              sps_guard = {}
              av_guard = {}
              avsb_g = {}
              avn_dg = {}

              def emit_loads(hp, ctx):
                  hb = hp % 2
                  lg = [("T", t_avend[hp - 2])] if hp - 2 in t_avend else []
                  asem = f"dA{hp % 2}"
                  P.op("sync", lambda h, hp=hp, hb=hb: h.dma_start_transpose(
                      qt[hb][:, :], d_q_s[:, hp * 128:(hp + 1) * 128]), waits=lg,
                      inc=asem, inc_n=16)
                  P.op("sync", lambda h, hp=hp, hb=hb: h.dma_start_transpose(
                      kt[hb][:, :], d_k_s[:, hp * 128:(hp + 1) * 128]), inc=asem, inc_n=16)
                  ld_h = None
                  for kb4 in range(8):
                      ld_h = P.op("sync", lambda h, hp=hp, hb=hb, kb4=kb4: h.dma_start(
                          vh[hb][:, kb4 * 512:(kb4 + 1) * 512].rearrange(
                              "p (kb d) -> p kb d", d=128),
                          d_v_s[kb4 * 512:(kb4 + 1) * 512,
                                hp * 128:(hp + 1) * 128].rearrange(
                              "(kb p) d -> p kb d", p=128)),
                          inc=asem, inc_n=16)
                  ctx[("ld", hp)] = ld_h

              def emit_front(hp, i, ctx):
                  hb = hp % 2
                  asem = f"dA{hp % 2}"
                  kcn = i // 2 + 1
                  eslot = i % 2
                  zb = 24 * eslot
                  s_e, v_e = {}, {}
                  for kc in range(kcn):
                      bb = kc % 2
                      diag = (kc == kcn - 1)
                      gq = [(asem, ctx[("ld", hp)])] if (i <= 1 and kc == 0) else []
                      gps_ = [sps_guard[bb]] if bb in sps_guard else []
                      eg = []
                      if kc == 0 and ("tr_done", hp, i - 2) in ctx:
                          eg = [("T", ctx[("tr_done", hp, i - 2)])]
                      elif kc == 0 and i < 2 and ("tr_last", hp - 1) in ctx:
                          eg = [("T", ctx[("tr_last", hp - 1)])]
                      t_s0 = P.op("tensor", lambda h, hb=hb, i=i, kc=kc, bb=bb: h.matmul(
                          s0ps[bb][:, :], qt[hb][0:64, i * 128:(i + 1) * 128],
                          kt[hb][0:64, kc * 512:(kc + 1) * 512], start=True, stop=True),
                          waits=gq + gps_ + eg, inc="T")
                      t_s1 = P.op("tensor", lambda h, hb=hb, i=i, kc=kc, bb=bb: h.matmul(
                          s1ps[bb][:, :], qt[hb][64:128, i * 128:(i + 1) * 128],
                          kt[hb][64:128, kc * 512:(kc + 1) * 512], start=True, stop=True),
                          inc="T")
                      if not diag:
                          s_e[(0, kc)] = P.op("scalar", lambda h, kc=kc, bb=bb, eslot=eslot, zb=zb: h.activation(
                              e0b[eslot][:, kc * 512:(kc + 1) * 512], s0ps[bb][:, :], AF.Exp,
                              accum_out=zc[:, zb + kc:zb + kc + 1]),
                              waits=[("T", t_s0)], inc="S")
                          s_e[(1, kc)] = P.op("scalar", lambda h, kc=kc, bb=bb, eslot=eslot, zb=zb: h.activation(
                              e1b[eslot][:, kc * 512:(kc + 1) * 512], s1ps[bb][:, :], AF.Exp,
                              accum_out=zc[:, zb + 8 + kc:zb + 9 + kc]),
                              waits=[("T", t_s1)], inc="S")
                          sps_guard[bb] = ("S", s_e[(1, kc)])
                      else:
                          par = i % 2
                          erg = []
                          if ("v_eraw", hp, i - 2) in ctx:
                              erg = [("V", ctx[("v_eraw", hp, i - 2)])]
                          elif i < 2 and ("v_eraw_last", hp - 1) in ctx:
                              erg = [("V", ctx[("v_eraw_last", hp - 1)])]
                          sr0 = P.op("scalar", lambda h, bb=bb, eslot=eslot: h.activation(
                              eraw[eslot][:, :], s0ps[bb][:, :], AF.Exp),
                              waits=[("T", t_s0)] + erg, inc="S")
                          v_e[(0, kc)] = P.op("vector", lambda h, kc=kc, par=par, eslot=eslot, zb=zb: h.scalar_tensor_tensor(
                              e0b[eslot][:, kc * 512:(kc + 1) * 512], eraw[eslot][:, :], 1.0,
                              maskb[:, par * 512:(par + 1) * 512], ALU.mult, ALU.mult,
                              accum_out=zc[:, zb + kc:zb + kc + 1]),
                              waits=[("S", sr0), ("dG", gm)], inc="V")
                          sr1 = P.op("scalar", lambda h, bb=bb, eslot=eslot: h.activation(
                              eraw[eslot][:, :], s1ps[bb][:, :], AF.Exp),
                              waits=[("T", t_s1), ("V", v_e[(0, kc)])], inc="S")
                          v_e[(1, kc)] = P.op("vector", lambda h, kc=kc, par=par, eslot=eslot, zb=zb: h.scalar_tensor_tensor(
                              e1b[eslot][:, kc * 512:(kc + 1) * 512], eraw[eslot][:, :], 1.0,
                              maskb[:, par * 512:(par + 1) * 512], ALU.mult, ALU.mult,
                              accum_out=zc[:, zb + 8 + kc:zb + 9 + kc]),
                              waits=[("S", sr1)], inc="V")
                          ctx[("v_eraw", hp, i)] = v_e[(1, kc)]
                          ctx[("v_eraw_last", hp)] = v_e[(1, kc)]
                          sps_guard[bb] = ("S", sr1)
                  wz = [("S", max(s_e.values()))] if s_e else []
                  if v_e:
                      wz.append(("V", v_e[(1, kcn - 1)]))
                  ctx[("exps_w", hp, i)] = list(wz)
                  P.op("vector", lambda h, kcn=kcn, zb=zb: h.reduce_sum(
                      zc[:, zb + 16:zb + 17], zc[:, zb:zb + kcn], axis=AX.X), waits=wz)
                  P.op("vector", lambda h, zb=zb: h.reciprocal(
                      zc[:, zb + 16:zb + 17], zc[:, zb + 16:zb + 17]))
                  P.op("vector", lambda h, kcn=kcn, zb=zb: h.reduce_sum(
                      zc[:, zb + 17:zb + 18], zc[:, zb + 8:zb + 8 + kcn], axis=AX.X))
                  P.op("vector", lambda h, zb=zb: h.reciprocal(
                      zc[:, zb + 17:zb + 18], zc[:, zb + 17:zb + 18]))
                  vz = P.op("vector", lambda h, zb=zb: h.tensor_mul(
                      zc[:, zb + 17:zb + 18], zc[:, zb + 17:zb + 18], lamneg[:, 0:1]), inc="V")
                  ctx[("v_z", hp, i)] = vz

              def emit_back_pe(hp, i, ctx):
                  hb = hp % 2
                  kcn = i // 2 + 1
                  ext = 512 * kcn
                  nkb = ext // 128
                  eslot = i % 2
                  ngr = (nkb + 7) // 8
                  exps_done = ctx[("exps_w", hp, i)]
                  slot = 0
                  v_evac = {}
                  t_av = {}
                  gidx = 0
                  for side in range(2):
                      eb = e0b[eslot] if side == 0 else e1b[eslot]
                      for gr in range(ngr):
                          kbs = list(range(gr * 8, min((gr + 1) * 8, nkb)))
                          pg_ = [("V", v_evac[gidx - 2])] if gidx - 2 in v_evac else []
                          aw = [("T", ctx[("t_avslot", slot)])] if ("t_avslot", slot) in ctx else []
                          t_tr = None
                          wv = []
                          for kb in kbs:
                              r = P.op("tensor", lambda h, kb=kb, gidx=gidx, eb=eb: h.transpose(
                                  ptrps[gidx % 2][:, (kb % 8) * 128:((kb % 8) + 1) * 128],
                                  eb[:, kb * 128:(kb + 1) * 128], identb[:, :]),
                                  waits=(exps_done + wv + pg_ + aw + [("dG", g_id)]) if kb == kbs[0] else [],
                                  inc="T" if kb == kbs[-1] else None)
                              if r is not None:
                                  t_tr = r
                          wdt = len(kbs) * 128
                          v_evac[gidx] = P.op("vector", lambda h, gidx=gidx, wdt=wdt, slot=slot: h.tensor_copy(
                              ptrb[:, slot * 1024: slot * 1024 + wdt],
                              ptrps[gidx % 2][:, 0:wdt]), waits=[("T", t_tr)], inc="V")
                          ag = av_guard.get(side)
                          last = None
                          for kb in kbs:
                              last = P.op("tensor", lambda h, kb=kb, hb=hb, side=side, slot=slot: h.matmul(
                                  avps[side][:, 0:128],
                                  ptrb[:, slot * 1024 + (kb % 8) * 128: slot * 1024 + (kb % 8 + 1) * 128],
                                  vh[hb][:, kb * 128:(kb + 1) * 128],
                                  start=(kb == 0), stop=(kb == nkb - 1)),
                                  waits=([("V", v_evac[gidx])] + ([ag] if (ag and kb == 0) else [])) if kb == kbs[0] else [],
                                  inc="T" if kb == kbs[-1] else None)
                          ctx[("t_avslot", slot)] = last if last is not None else P.cnt["T"]
                          if last is not None:
                              t_av[side] = last
                          slot = (slot + 1) % 4
                          gidx += 1
                  ctx[("t_av", hp, i)] = (t_av[0], t_av[1])
                  ctx[("tr_done", hp, i)] = t_av[1]
                  ctx[("tr_last", hp)] = t_av[1]

              def emit_back_post(hp, i, ctx):
                  ab_ = i % 2
                  zb = 24 * ab_
                  ta0, ta1 = ctx[("t_av", hp, i)]
                  ag2 = [avsb_g[ab_]] if ab_ in avsb_g else []
                  c1 = P.op("vector", lambda h, ab_=ab_, zb=zb: h.tensor_scalar(
                      avsb[ab_][:, :], avps[0][:, 0:128], zc[:, zb + 16:zb + 17],
                      None, ALU.mult), waits=[("T", ta0)] + ag2, inc="V")
                  c2 = P.op("vector", lambda h, ab_=ab_, zb=zb: h.scalar_tensor_tensor(
                      avsb[ab_][:, :], avps[1][:, 0:128], zc[:, zb + 17:zb + 18],
                      avsb[ab_][:, :], ALU.mult, ALU.add), waits=[("T", ta1)], inc="V")
                  av_guard[0] = ("V", c1)
                  av_guard[1] = ("V", c2)
                  P.op("vector", lambda h, ab_=ab_: h.reduce_sum(
                      zc[:, 44:45], avsb[ab_][:, :], axis=AX.X))
                  sq_ = P.op("scalar", lambda h, ab_=ab_: h.activation(
                      avsq[:, :], avsb[ab_][:, :], AF.Square,
                      accum_out=zc[:, 45:46]), waits=[("V", c2)], inc="S")
                  P.op("vector", lambda h: h.tensor_scalar(
                      zc[:, 44:45], zc[:, 44:45], 1.0 / 128, None, ALU.mult))
                  P.op("vector", lambda h: h.tensor_scalar(
                      zc[:, 45:46], zc[:, 45:46], 1.0 / 128, None, ALU.mult),
                      waits=[("S", sq_)])
                  P.op("vector", lambda h: h.tensor_scalar(
                      zc[:, 46:47], zc[:, 44:45], zc[:, 44:45], None, ALU.mult))
                  vv_ = P.op("vector", lambda h: h.tensor_sub(
                      zc[:, 45:46], zc[:, 45:46], zc[:, 46:47]), inc="V")
                  P.op("scalar", lambda h: h.activation(
                      zc[:, 45:46], zc[:, 45:46], AF.Ln, bias=constb[:, 0:1]),
                      waits=[("V", vv_)], inc="S")
                  sd_ = P.op("scalar", lambda h: h.activation(
                      zc[:, 45:46], zc[:, 45:46], AF.Exp, scale=constb[:, 3:4]), inc="S")
                  P.op("vector", lambda h: h.tensor_scalar(
                      zc[:, 45:46], zc[:, 45:46], 1.0 - LAM_INIT, None, ALU.mult),
                      waits=[("S", sd_)])
                  P.op("vector", lambda h: h.tensor_scalar(
                      zc[:, 46:47], zc[:, 45:46], -1.0, None, ALU.mult))
                  P.op("vector", lambda h: h.tensor_scalar(
                      zc[:, 46:47], zc[:, 44:45], zc[:, 46:47], None, ALU.mult))
                  ag3 = [(f"dN{i % 2}", avn_dg[i % 2])] if i % 2 in avn_dg else []
                  s_avn = P.op("vector", lambda h, ab_=ab_, i=i: h.tensor_scalar(
                      avnb[i % 2][:, :], avsb[ab_][:, :], zc[:, 45:46],
                      zc[:, 46:47], ALU.mult, ALU.add),
                      waits=ag3, inc="V")
                  avsb_g[ab_] = ("V", s_avn)
                  avn_dg[i % 2] = P.op("sync", lambda h, i=i, hp=hp: h.dma_start(
                      d_avn_s[i * 128:(i + 1) * 128, hp * 128:(hp + 1) * 128],
                      avnb[i % 2][:, :]), waits=[("V", s_avn)], inc=f"dN{i % 2}", inc_n=16)

              ctx = {}
              emit_loads(0, ctx)
              emit_loads(1, ctx)
              for hp in range(NHP):
                  for i in range(16):
                      if i >= 2:
                          emit_back_post(hp, i - 2, ctx)
                      emit_front(hp, i, ctx)
                      if i >= 1:
                          emit_back_pe(hp, i - 1, ctx)
                  emit_back_post(hp, 14, ctx)
                  emit_back_pe(hp, 15, ctx)
                  t_avend[hp] = P.cnt["T"]
                  if hp + 2 < NHP:
                      emit_loads(hp + 2, ctx)
                  emit_back_post(hp, 15, ctx)
              P.barrier()
              P.flush()


        with contextlib.ExitStack() as ph:
          if phases >= 5:
              PE = ph.enter_context
              avnT = [PE(nc.sbuf_tensor(f"pr_avnT{h2}", [128, TOK], BF16,
                                        is_transpose_dest=True)) for h2 in range(8)]
              wpb = PE(nc.sbuf_tensor("wpb", [128, 8 * C], BF16))
              xb2 = [PE(nc.sbuf_tensor(f"pr_x{j}", [128, C], F32)) for j in range(2)]
              gb = [PE(nc.sbuf_tensor(f"pr_g{j}", [128, C], F32)) for j in range(2)]
              ob = [PE(nc.sbuf_tensor(f"pr_o{j}", [128, C], F32)) for j in range(2)]
              tfb = [PE(nc.sbuf_tensor(f"pr_t{j}", [128, 512], F32)) for j in range(2)]
              prps = [PE(nc.psum_tensor(f"prps{j}", [128, 512], F32)) for j in range(2)]

              tla = None
              for h2 in range(8):
                  tla = P.op("sync", lambda h, h2=h2: h.dma_start_transpose(
                      avnT[h2][:, :], d_avn_s[:, h2 * 128:(h2 + 1) * 128]),
                      inc="dT", inc_n=16)
              g_wp = None
              for k in range(8):
                  g_wp = P.op("gpsimd", lambda h, k=k: h.dma_start(
                      wpb[:, k * C:(k + 1) * C], d_wp[k * 128:(k + 1) * 128, :]),
                      inc="dG", inc_n=16)
              v_out, st_out = {}, {}
              for m in range(16):
                  xg = [(f"dO{m % 2}", st_out[m - 2])] if m - 2 in st_out else []
                  lx = P.op("sync", lambda h, m=m: h.dma_start(
                      xb2[m % 2][:, :], d_x_own[m * 128:(m + 1) * 128, :]), waits=xg,
                      inc=f"dA{m % 2}", inc_n=16)
                  gg_ = [("V", v_out[(m - 2, 1)])] if (m - 2, 1) in v_out else []
                  sg_ = P.op("scalar", lambda h, m=m: h.activation(
                      gb[m % 2][:, :], y_sb[:, m * 1024:(m + 1) * 1024], AF.Silu),
                      waits=gg_, inc="S")
                  for n in range(2):
                      pbf = (2 * m + n) % 2
                      pg_ = [("V", v_out[(m - 1, n)])] if (m - 1, n) in v_out else []
                      t_pr = None
                      for h2 in range(8):
                          r = P.op("tensor", lambda h, h2=h2, m=m, n=n, pbf=pbf: h.matmul(
                              prps[pbf][:, :], avnT[h2][:, m * 128:(m + 1) * 128],
                              wpb[:, h2 * C + n * 512: h2 * C + n * 512 + 512],
                              start=(h2 == 0), stop=(h2 == 7)),
                              waits=([("dT", tla), ("dG", g_wp)] + pg_) if h2 == 0 else [],
                              inc="T" if h2 == 7 else None)
                          if r is not None:
                              t_pr = r
                      og = [(f"dO{m % 2}", st_out[m - 2])] if (n == 0 and m - 2 in st_out) else []
                      P.op("vector", lambda h, n=n, pbf=pbf, m=m: h.tensor_mul(
                          tfb[n][:, :], prps[pbf][:, :], gb[m % 2][:, n * 512:(n + 1) * 512]),
                          waits=[("T", t_pr), ("S", sg_)])
                      v_out[(m, n)] = P.op("vector", lambda h, m=m, n=n: h.tensor_add(
                          ob[m % 2][:, n * 512:(n + 1) * 512], tfb[n][:, :],
                          xb2[m % 2][:, n * 512:(n + 1) * 512]),
                          waits=[(f"dA{m % 2}", lx)] + og, inc="V")
                  st_out[m] = P.op("sync", lambda h, m=m: h.dma_start(
                      d_out[m * 128:(m + 1) * 128, :], ob[m % 2][:, :]),
                      waits=[("V", v_out[(m, 1)])], inc=f"dO{m % 2}", inc_n=16)

              if debug:
                  vd = P.op("vector", lambda h: h.tensor_copy(
                      tfb[0][:, 0:128], wtok_sb[:, :]), inc="V")
                  P.op("sync", lambda h: h.dma_start(dbg["dbg_wtok"], tfb[0][:, 0:128]),
                       waits=[("V", vd)], inc="dT", inc_n=16)
                  P.op("sync", lambda h: h.dma_start(dbg["dbg_y"], y_sb[:, :]),
                       inc="dT", inc_n=16)
                  for name, src in [("dbg_xn_own", d_xn_o), ("dbg_xn_ctx", d_xn_c),
                                    ("dbg_q", d_q_s), ("dbg_k", d_k_s), ("dbg_v", d_v_s),
                                    ("dbg_avn", d_avn_s)]:
                      P.op("sync", lambda h, name=name, src=src: h.dma_start(
                          dbg[name], src), inc="dT", inc_n=16)
              P.barrier()
              P.flush()

    return nc



def _tables(off):
    inv_freq = (10000.0 ** (-np.arange(0, HD, 2, dtype=np.float32) / HD)).astype(np.float32)

    def tab(pos):
        fr = np.outer(pos.astype(np.float32), inv_freq)
        c = np.cos(fr).astype(np.float32)
        s = np.sin(fr).astype(np.float32)
        cd = np.concatenate([c, c], axis=1)
        sd = np.concatenate([s, s], axis=1)
        return (np.ascontiguousarray(np.tile(cd, (1, 8))),
                np.ascontiguousarray(np.tile(sd, (1, 8))))

    c_o, s_o = tab(np.arange(off, 2 * TOK, 2))
    c_c, s_c = tab(np.arange(T))
    return c_o, s_o, c_c, s_c


def _mask(off):
    r = np.arange(128)[:, None]
    j = np.arange(512)[None, :]
    m = np.zeros((128, 1024), np.float32)
    m[:, 0:512] = (j <= 2 * r + off)
    m[:, 512:1024] = (j <= 256 + 2 * r + off)
    return m.astype(ml_dtypes.bfloat16)


_NC_CACHE = {}


def kernel(L0_attn_x, L0_attn_Wq, L0_attn_Wk, L0_attn_Wv, L0_attn_Wproj,
           L0_attn_lq1, L0_attn_lk1, L0_attn_lq2, L0_attn_lk2,
           L0_ffn_gate_w, L0_ffn_w1, L0_ffn_w2, L0_ffn_w3, _debug=False):
    f32 = np.float32
    x = np.ascontiguousarray(np.asarray(L0_attn_x, dtype=f32))
    wq_t = np.ascontiguousarray(np.asarray(L0_attn_Wq, dtype=f32).T)
    wk_t = np.ascontiguousarray(np.asarray(L0_attn_Wk, dtype=f32).T)
    wv_t = np.ascontiguousarray(np.asarray(L0_attn_Wv, dtype=f32).T)
    wp_t = np.ascontiguousarray(np.asarray(L0_attn_Wproj, dtype=f32).T)
    gate_t = np.ascontiguousarray(np.asarray(L0_ffn_gate_w, dtype=f32).T)
    w1_t = np.ascontiguousarray(np.asarray(L0_ffn_w1, dtype=f32).transpose(0, 2, 1))
    w3_t = np.ascontiguousarray(np.asarray(L0_ffn_w3, dtype=f32).transpose(0, 2, 1))
    w2_t = np.ascontiguousarray(np.asarray(L0_ffn_w2, dtype=f32).transpose(0, 2, 1))
    lqk = np.ascontiguousarray(np.concatenate(
        [np.asarray(a, dtype=f32).reshape(-1) for a in
         (L0_attn_lq1, L0_attn_lk1, L0_attn_lq2, L0_attn_lk2)])[None, :])
    ident = np.eye(128, dtype=f32).astype(ml_dtypes.bfloat16)

    key = ("prog", _debug)
    if key not in _NC_CACHE:
        _NC_CACHE[key] = build_program(debug=_debug)
    nc = _NC_CACHE[key]

    tabs = {off: _tables(off) for off in (0, 1)}
    masks = {off: _mask(off) for off in (0, 1)}
    in_maps = []
    for c in range(8):
        b, off = c // 2, c % 2
        c_o, s_o, c_c, s_c = tabs[off]
        in_maps.append({
            "x_own": np.ascontiguousarray(x[b, off::2, :]),
            "x_ctx": np.ascontiguousarray(x[b]),
            "wq_t": wq_t, "wk_t": wk_t, "wv_t": wv_t, "wproj_t": wp_t,
            "gate_t": gate_t, "w1_t": w1_t, "w3_t": w3_t, "w2_t": w2_t,
            "lqk": lqk, "cos_own": c_o, "sin_own": s_o,
            "cos_ctx": c_c, "sin_ctx": s_c, "mask": masks[off], "ident": ident,
        })

    res = run_bass_kernel_spmd(nc, in_maps, core_ids=list(range(8)))
    out = np.empty((B, T, C), np.float32)
    for c in range(8):
        b, off = c // 2, c % 2
        out[b, off::2, :] = res.results[c]["out"]
    if _debug:
        return out, res
    return out


# version 07
# speedup vs baseline: 1.1972x; 1.1972x over previous
import contextlib
import math
import os

import numpy as np
import ml_dtypes

import concourse.bass as bass
import concourse.mybir as mybir
from concourse.bass_utils import run_bass_kernel_spmd

F32 = mybir.dt.float32
BF16 = mybir.dt.bfloat16
AF = mybir.ActivationFunctionType
ALU = mybir.AluOpType
AX = mybir.AxisListType

B, T, C = 4, 4096, 1024
TOK = 2048
NHP = 8
HD = 64
LAM_INIT = 0.8 - 0.6 * math.exp(-0.3 * 4)
RMS_EPS = float(np.finfo(np.float32).eps)
LN_EPS = 1e-5
NEXP, FF = 8, 4096
FG = 1024
NGRP = FF // FG
TBS = 256
NTB = TOK // TBS
JIT2 = 0.02
NEG_BIG = -1e30

ENGS = ("sync", "gpsimd", "tensor", "vector", "scalar")


class Prog:

    def __init__(self, blk, sems):
        self.blk = blk
        self.sems = sems
        self.cnt = {k: 0 for k in sems}
        self.waited = {}
        self.q = {e: [] for e in ENGS}

    SELF_SEM = {"vector": "V", "scalar": "S"}

    def op(self, eng, fn, waits=(), inc=None, inc_n=1):
        waits = [(s, v) for (s, v) in waits if v and v > 0]
        own = self.SELF_SEM.get(eng)
        if own is not None and fn is not None:
            if self.cnt[own] > 0:
                waits.append((own, self.cnt[own]))
            if inc is None:
                inc = own
        if inc is not None:
            self.cnt[inc] += inc_n
        self.q[eng].append((waits, fn, inc, inc_n))
        return self.cnt[inc] if inc is not None else None

    def barrier(self):
        vals = [(k, v) for k, v in self.cnt.items() if v > 0]
        for e in ENGS:
            self.op(e, None, waits=vals)

    def flush(self):
        for e in ENGS:
            ops = self.q[e]
            if not ops:
                continue
            sec = getattr(self.blk, e)

            def body(h, eng_name=e, ops_local=list(ops)):
                for waits, fn, inc, inc_n in ops_local:
                    for s, v in waits:
                        key = (eng_name, s)
                        if self.waited.get(key, 0) < v:
                            h.wait_ge(self.sems[s], v)
                            self.waited[key] = v
                    if fn is not None:
                        instr = fn(h)
                        if inc is not None:
                            instr.then_inc(self.sems[inc], inc_n)

            sec(body)
            self.q[e] = []


def build_program(debug=False, phases=5):
    nc = bass.Bass("TRN2", target_bir_lowering=False)

    d_x_own = nc.dram_tensor("x_own", [TOK, C], F32, kind="ExternalInput").ap()
    d_x_ctx = nc.dram_tensor("x_ctx", [T, C], F32, kind="ExternalInput").ap()
    d_wq = nc.dram_tensor("wq_t", [C, C], F32, kind="ExternalInput").ap()
    d_wk = nc.dram_tensor("wk_t", [C, C], F32, kind="ExternalInput").ap()
    d_wv = nc.dram_tensor("wv_t", [C, C], F32, kind="ExternalInput").ap()
    d_wp = nc.dram_tensor("wproj_t", [C, C], F32, kind="ExternalInput").ap()
    d_gate = nc.dram_tensor("gate_t", [C, NEXP], F32, kind="ExternalInput").ap()
    d_w1 = nc.dram_tensor("w1_t", [NEXP, C, FF], F32, kind="ExternalInput").ap()
    d_w3 = nc.dram_tensor("w3_t", [NEXP, C, FF], F32, kind="ExternalInput").ap()
    d_w2 = nc.dram_tensor("w2_t", [NEXP, FF, C], F32, kind="ExternalInput").ap()
    d_lqk = nc.dram_tensor("lqk", [1, 4 * HD], F32, kind="ExternalInput").ap()
    d_cos_o = nc.dram_tensor("cos_own", [TOK, 512], F32, kind="ExternalInput").ap()
    d_sin_o = nc.dram_tensor("sin_own", [TOK, 512], F32, kind="ExternalInput").ap()
    d_cos_c = nc.dram_tensor("cos_ctx", [T, 512], F32, kind="ExternalInput").ap()
    d_sin_c = nc.dram_tensor("sin_ctx", [T, 512], F32, kind="ExternalInput").ap()
    d_mask = nc.dram_tensor("mask", [128, 1024], BF16, kind="ExternalInput").ap()
    d_ident = nc.dram_tensor("ident", [128, 128], BF16, kind="ExternalInput").ap()

    d_out = nc.dram_tensor("out", [TOK, C], F32, kind="ExternalOutput").ap()

    d_xn_o = nc.dram_tensor("xn_own_s", [TOK, C], BF16, kind="Internal").ap()
    d_xn_c = nc.dram_tensor("xn_ctx_s", [T, C], BF16, kind="Internal").ap()
    d_q_s = nc.dram_tensor("q_s", [TOK, C], BF16, kind="Internal").ap()
    d_k_s = nc.dram_tensor("k_s", [T, C], BF16, kind="Internal").ap()
    d_v_s = nc.dram_tensor("v_s", [T, C], BF16, kind="Internal").ap()
    d_avn_s = nc.dram_tensor("avn_s", [TOK, C], BF16, kind="Internal").ap()

    dbg = {}
    if debug:
        for name, shp, dt in [
            ("dbg_xn_own", [TOK, C], BF16), ("dbg_xn_ctx", [T, C], BF16),
            ("dbg_q", [TOK, C], BF16), ("dbg_k", [T, C], BF16), ("dbg_v", [T, C], BF16),
            ("dbg_wtok", [128, 128], F32), ("dbg_y", [128, 16384], F32),
            ("dbg_avn", [TOK, C], BF16),
        ]:
            dbg[name] = nc.dram_tensor(name, shp, dt, kind="ExternalOutput").ap()

    with contextlib.ExitStack() as root:
        E = root.enter_context
        semnames = ("T", "V", "S", "dG", "dG0", "dG1", "dT",
                    "dX0", "dX1", "dX2", "dW0", "dW1",
                    "dTab0", "dTab1", "dQ0", "dQ1",
                    "dA0", "dA1", "dN0", "dN1", "dO0", "dO1")
        sems = {k: E(nc.semaphore(k)) for k in semnames}
        blk = E(nc.Block())
        P = Prog(blk, sems)

        wtok_sb = E(nc.sbuf_tensor("wtok_sb", [128, 128], F32))
        lamneg = E(nc.sbuf_tensor("lamneg", [128, 1], F32))
        y_sb = E(nc.sbuf_tensor("y_sb", [128, 16 * 1024], F32))
        identb = E(nc.sbuf_tensor("identb", [128, 128], BF16))
        constb = E(nc.sbuf_tensor("constb", [128, 4], F32))

        g_id = P.op("gpsimd", lambda h: h.dma_start(identb[:, :], d_ident),
                    inc="dG", inc_n=16)
        P.op("vector", lambda h: h.memset(constb[:, 0:1], LN_EPS))
        P.op("vector", lambda h: h.memset(constb[:, 1:2], RMS_EPS))
        P.op("vector", lambda h: h.memset(constb[:, 2:3], 1.0 / HD))
        P.op("vector", lambda h: h.memset(constb[:, 3:4], -0.5), inc="V")

        with contextlib.ExitStack() as ph:
          if phases >= 1:
              PE = ph.enter_context
              xb = [PE(nc.sbuf_tensor(f"p0_x{j}", [128, C], F32)) for j in range(3)]
              sqs = [PE(nc.sbuf_tensor(f"p0_sq{j}", [128, C], F32)) for j in range(2)]
              xnb = [PE(nc.sbuf_tensor(f"p0_xn{j}", [128, C], BF16)) for j in range(2)]
              st = [PE(nc.sbuf_tensor(f"p0_st{j}", [128, 8], F32)) for j in range(2)]

              tiles = [("own", t) for t in range(16)] + [("ctx", t) for t in range(32)]
              s_xn, sp_done, v_fin, ssq_d = {}, {}, {}, {}
              for idx, (kind, t) in enumerate(tiles):
                  src = d_x_own if kind == "own" else d_x_ctx
                  dst = d_xn_o if kind == "own" else d_xn_c
                  b3, b2 = idx % 3, idx % 2
                  x_t, sq_t, xn_t, s_t = xb[b3], sqs[b2], xnb[b2], st[b2]
                  w_guard = []
                  if idx - 3 in s_xn:
                      w_guard = [("V", s_xn[idx - 3]), ("S", ssq_d[idx - 3])]
                  xsem = f"dX{idx % 3}"
                  ld = P.op("sync", lambda h, a=x_t, s=src, t=t: h.dma_start(
                      a[:, :], s[t * 128:(t + 1) * 128, :]), waits=w_guard, inc=xsem, inc_n=16)
                  g2 = [("V", v_fin[idx - 2])] if idx - 2 in v_fin else []
                  P.op("vector", lambda h, a=s_t, x=x_t: h.reduce_sum(
                      a[:, 0:1], x[:, :], axis=AX.X), waits=[(xsem, ld)] + g2)
                  ssq = P.op("scalar", lambda h, a=sq_t, x=x_t, s=s_t: h.activation(
                      a[:, :], x[:, :], AF.Square, accum_out=s[:, 1:2]),
                      waits=[(xsem, ld)] + g2, inc="S")
                  ssq_d[idx] = ssq
                  P.op("vector", lambda h, s=s_t: h.tensor_scalar(
                      s[:, 2:3], s[:, 0:1], 1.0 / C, None, ALU.mult))
                  P.op("vector", lambda h, s=s_t: h.tensor_scalar(
                      s[:, 3:4], s[:, 1:2], 1.0 / C, None, ALU.mult), waits=[("S", ssq)])
                  P.op("vector", lambda h, s=s_t: h.tensor_scalar(
                      s[:, 4:5], s[:, 2:3], s[:, 2:3], None, ALU.mult))
                  vv = P.op("vector", lambda h, s=s_t: h.tensor_sub(
                      s[:, 5:6], s[:, 3:4], s[:, 4:5]), inc="V")
                  P.op("scalar", lambda h, s=s_t: h.activation(
                      s[:, 6:7], s[:, 5:6], AF.Ln, bias=constb[:, 0:1]),
                      waits=[("V", vv)], inc="S")
                  sd = P.op("scalar", lambda h, s=s_t: h.activation(
                      s[:, 6:7], s[:, 6:7], AF.Exp, scale=constb[:, 3:4]), inc="S")
                  P.op("vector", lambda h, s=s_t: h.tensor_scalar(
                      s[:, 7:8], s[:, 6:7], -1.0, None, ALU.mult), waits=[("S", sd)])
                  vn = P.op("vector", lambda h, s=s_t: h.tensor_scalar(
                      s[:, 7:8], s[:, 2:3], s[:, 7:8], None, ALU.mult), inc="V")
                  v_fin[idx] = vn
                  wsem = f"dW{idx % 2}"
                  xg = [(wsem, sp_done[idx - 2])] if idx - 2 in sp_done else []
                  sx = P.op("vector", lambda h, a=xn_t, x=x_t, s=s_t: h.tensor_scalar(
                      a[:, :], x[:, :], s[:, 6:7], s[:, 7:8], ALU.mult, ALU.add),
                      waits=xg, inc="V")
                  s_xn[idx] = sx
                  sp_done[idx] = P.op("sync", lambda h, a=xn_t, d=dst, t=t: h.dma_start(
                      d[t * 128:(t + 1) * 128, :], a[:, :]), waits=[("V", sx)],
                      inc=wsem, inc_n=16)
              P.barrier()
              P.flush()

        with contextlib.ExitStack() as p1:
          if phases >= 2:
              P1E = p1.enter_context
              qps = [P1E(nc.psum_tensor(f"qps{j}", [128, 512], F32)) for j in range(2)]
              ctab = [P1E(nc.sbuf_tensor(f"ctab{j}", [128, 512], F32)) for j in range(2)]
              stab = [P1E(nc.sbuf_tensor(f"stab{j}", [128, 512], F32)) for j in range(2)]
              sq5 = [P1E(nc.sbuf_tensor(f"sq5{j}", [128, 512], F32)) for j in range(2)]
              qn5 = [P1E(nc.sbuf_tensor(f"qn5{j}", [128, 512], F32)) for j in range(2)]
              ab5 = [P1E(nc.sbuf_tensor(f"ab5{j}", [128, 512], F32)) for j in range(2)]
              bb5 = [P1E(nc.sbuf_tensor(f"bb5{j}", [128, 512], F32)) for j in range(2)]
              s8 = [P1E(nc.sbuf_tensor(f"s8_{j}", [128, 16], F32)) for j in range(2)]
              qrb = [P1E(nc.sbuf_tensor(f"qrb{j}", [128, 512], BF16)) for j in range(2)]

              s_post, v_post, sp_q = {}, {}, {}

              def qkv_part(tag, xnT, n_m, wsb, gw, tlw, dsp, dcos, dsin, has_rope, is_q):
                  v_tab = {}
                  for m in range(n_m):
                      ct = None
                      tsem = f"dTab{m % 2}"
                      if has_rope:
                          tg2 = [("V", v_tab[m - 2])] if m - 2 in v_tab else []
                          P.op("sync", lambda h, m=m, dcos=dcos: h.dma_start(
                              ctab[m % 2][:, :], dcos[m * 128:(m + 1) * 128, :]),
                              waits=tg2, inc=tsem, inc_n=16)
                          ct = P.op("sync", lambda h, m=m, dsin=dsin: h.dma_start(
                              stab[m % 2][:, :], dsin[m * 128:(m + 1) * 128, :]),
                              inc=tsem, inc_n=16)
                      for n in range(2):
                          bi = (m * 2 + n) % 2
                          pq = qps[bi]
                          guard = []
                          if bi in s_post:
                              guard.append(("S", s_post[bi]))
                          if bi in v_post:
                              guard.append(("V", v_post[bi]))
                          tq = None
                          for k in range(8):
                              r = P.op("tensor", lambda h, k=k, m=m, n=n, pq=pq, xnT=xnT, wsb=wsb: h.matmul(
                                  pq[:, :], xnT[k][:, m * 128:(m + 1) * 128],
                                  wsb[:, k * C + n * 512: k * C + n * 512 + 512],
                                  start=(k == 0), stop=(k == 7)),
                                  waits=([("dG", gw), ("dT", tlw)] + guard) if k == 0 else [],
                                  inc="T" if k == 7 else None)
                              if r is not None:
                                  tq = r
                          if has_rope:
                              b2 = bi
                              ssq = P.op("scalar", lambda h, pq=pq, b2=b2: h.activation(
                                  sq5[b2][:, :], pq[:, :], AF.Square), waits=[("T", tq)], inc="S")
                              vred = None
                              for j in range(8):
                                  vred = P.op("vector", lambda h, j=j, b2=b2: h.reduce_sum(
                                      s8[b2][:, j:j + 1], sq5[b2][:, j * 64:(j + 1) * 64],
                                      axis=AX.X), waits=[("S", ssq)] if j == 0 else [],
                                      inc="V" if j == 7 else None)
                              P.op("scalar", lambda h, b2=b2: h.activation(
                                  s8[b2][:, 8:16], s8[b2][:, 0:8], AF.Ln,
                                  scale=constb[:, 2:3], bias=constb[:, 1:2]),
                                  waits=[("V", vred)], inc="S")
                              sd8 = P.op("scalar", lambda h, b2=b2: h.activation(
                                  s8[b2][:, 8:16], s8[b2][:, 8:16], AF.Exp,
                                  scale=constb[:, 3:4]), inc="S")
                              if is_q:
                                  P.op("vector", lambda h, b2=b2: h.tensor_scalar(
                                      s8[b2][:, 8:16], s8[b2][:, 8:16], 0.125, None, ALU.mult),
                                      waits=[("S", sd8)])
                              vpost = None
                              for j in range(8):
                                  vpost = P.op("vector", lambda h, j=j, pq=pq, b2=b2: h.tensor_scalar(
                                      qn5[b2][:, j * 64:(j + 1) * 64], pq[:, j * 64:(j + 1) * 64],
                                      s8[b2][:, 8 + j:9 + j], None, ALU.mult),
                                      waits=[("S", sd8)] if j == 0 else [],
                                      inc="V" if j == 7 else None)
                              v_post[b2] = vpost
                              s_post[b2] = ssq
                              P.op("vector", lambda h, b2=b2, m=m: h.tensor_mul(
                                  ab5[b2][:, :], qn5[b2][:, :], ctab[m % 2][:, :]),
                                  waits=[(tsem, ct)])
                              vr = P.op("vector", lambda h, b2=b2, m=m: h.tensor_mul(
                                  bb5[b2][:, :], qn5[b2][:, :], stab[m % 2][:, :]), inc="V")
                              v_tab[m] = vr
                              spg = [(f"dQ{b2}", sp_q[b2])] if b2 in sp_q else []
                              P.op("vector", lambda h, b2=b2: h.tensor_add(
                                  qrb[b2][:, :].rearrange("p (g d) -> p g d", d=64)[:, :, 0:32],
                                  ab5[b2][:, :].rearrange("p (g d) -> p g d", d=64)[:, :, 0:32],
                                  bb5[b2][:, :].rearrange("p (g d) -> p g d", d=64)[:, :, 32:64]),
                                  waits=spg)
                              vr2 = P.op("vector", lambda h, b2=b2: h.tensor_sub(
                                  qrb[b2][:, :].rearrange("p (g d) -> p g d", d=64)[:, :, 32:64],
                                  ab5[b2][:, :].rearrange("p (g d) -> p g d", d=64)[:, :, 32:64],
                                  bb5[b2][:, :].rearrange("p (g d) -> p g d", d=64)[:, :, 0:32]),
                                  inc="V")
                              sp_q[b2] = P.op("sync", lambda h, b2=b2, m=m, n=n, dsp=dsp: h.dma_start(
                                  dsp[m * 128:(m + 1) * 128, n * 512:n * 512 + 512],
                                  qrb[b2][:, :]), waits=[("V", vr2)], inc=f"dQ{b2}", inc_n=16)
                          else:
                              spg = [(f"dQ{bi}", sp_q[bi])] if bi in sp_q else []
                              sv = P.op("scalar", lambda h, pq=pq, bi=bi: h.activation(
                                  qrb[bi][:, :], pq[:, :], AF.Copy),
                                  waits=[("T", tq)] + spg, inc="S")
                              s_post[bi] = sv
                              v_post.pop(bi, None)
                              sp_q[bi] = P.op("sync", lambda h, bi=bi, m=m, n=n, dsp=dsp: h.dma_start(
                                  dsp[m * 128:(m + 1) * 128, n * 512:n * 512 + 512],
                                  qrb[bi][:, :]), waits=[("S", sv)], inc=f"dQ{bi}", inc_n=16)

              with contextlib.ExitStack() as pq_scope:
                  QE = pq_scope.enter_context
                  xnT_o = [QE(nc.sbuf_tensor(f"xnT_o{k}", [128, TOK], BF16,
                                             is_transpose_dest=True)) for k in range(8)]
                  wqb = QE(nc.sbuf_tensor("wqb", [128, 8 * C], BF16))
                  gateb = QE(nc.sbuf_tensor("gateb", [128, 64], BF16))
                  lqkb = QE(nc.sbuf_tensor("lqkb", [128, 4 * HD], F32))
                  gps = [QE(nc.psum_tensor(f"gps{j}", [128, 512], F32)) for j in range(2)]
                  rt = QE(nc.sbuf_tensor("rt", [128, 8], F32))
                  r8 = QE(nc.sbuf_tensor("r8", [128, 72], F32))
                  lt = QE(nc.sbuf_tensor("lt", [128, HD], F32))
                  lam_c = QE(nc.sbuf_tensor("lam_c", [128, 2], F32))

                  tl_o = None
                  for k in range(8):
                      tl_o = P.op("sync", lambda h, k=k: h.dma_start_transpose(
                          xnT_o[k][:, :], d_xn_o[:, k * 128:(k + 1) * 128]),
                          inc="dT", inc_n=16)
                  g_wq = None
                  for k in range(8):
                      g_wq = P.op("gpsimd", lambda h, k=k: h.dma_start(
                          wqb[:, k * C:(k + 1) * C], d_wq[k * 128:(k + 1) * 128, :]),
                          inc="dG", inc_n=16)
                  g_gate = P.op("gpsimd", lambda h: h.dma_start(
                      gateb[:, :].rearrange("p (k e) -> p k e", e=NEXP),
                      d_gate.rearrange("(k p) e -> p k e", p=128)), inc="dG", inc_n=16)
                  g_lqk = P.op("gpsimd", lambda h: h.dma_start(
                      lqkb[:, :], d_lqk.broadcast_to([128, 4 * HD])), inc="dG", inc_n=16)
                  g_wq = g_gate = g_lqk

                  P.op("vector", lambda h: h.tensor_mul(
                      lt[:, :], lqkb[:, 0:64], lqkb[:, 64:128]), waits=[("dG", g_lqk)])
                  P.op("vector", lambda h: h.reduce_sum(lam_c[:, 0:1], lt[:, :], axis=AX.X))
                  P.op("vector", lambda h: h.tensor_mul(
                      lt[:, :], lqkb[:, 128:192], lqkb[:, 192:256]))
                  vl = P.op("vector", lambda h: h.reduce_sum(
                      lam_c[:, 1:2], lt[:, :], axis=AX.X), inc="V")
                  P.op("scalar", lambda h: h.activation(
                      lam_c[:, 0:1], lam_c[:, 0:1], AF.Exp), waits=[("V", vl)], inc="S")
                  sl = P.op("scalar", lambda h: h.activation(
                      lam_c[:, 1:2], lam_c[:, 1:2], AF.Exp), inc="S")
                  P.op("vector", lambda h: h.tensor_sub(
                      lamneg[:, 0:1], lam_c[:, 0:1], lam_c[:, 1:2]), waits=[("S", sl)])
                  P.op("vector", lambda h: h.tensor_scalar(
                      lamneg[:, 0:1], lamneg[:, 0:1], LAM_INIT, -1.0, ALU.add, ALU.mult))

                  v_wtok, s_e2g = {}, {}
                  for m in range(16):
                      pg = gps[m % 2]
                      guard = []
                      if m - 2 in v_wtok:
                          guard = [("V", v_wtok[m - 2]), ("S", s_e2g[m - 2])]
                      tg = None
                      for k in range(8):
                          r = P.op("tensor", lambda h, k=k, m=m, pg=pg: h.matmul(
                              pg[:, 0:8], xnT_o[k][:, m * 128:(m + 1) * 128],
                              gateb[:, k * 8:(k + 1) * 8], start=(k == 0), stop=(k == 7)),
                              waits=([("dT", tl_o), ("dG", g_gate)] + guard) if k == 0 else [],
                              inc="T" if k == 7 else None)
                          if r is not None:
                              tg = r
                      ag = [("V", v_wtok[m - 1])] if m - 1 in v_wtok else []
                      P.op("vector", lambda h, pg=pg: h.reduce_max(
                          rt[:, 0:1], pg[:, 0:8], axis=AX.X), waits=[("T", tg)])
                      P.op("vector", lambda h, pg=pg: h.tensor_scalar(
                          r8[:, 0:8], pg[:, 0:8], rt[:, 0:1], None, ALU.is_equal))
                      sab = P.op("scalar", lambda h, pg=pg: h.activation(
                          r8[:, 8:16], pg[:, 0:8], AF.Abs), waits=[("T", tg)] + ag, inc="S")
                      P.op("vector", lambda h: h.tensor_scalar(
                          r8[:, 16:24], r8[:, 8:16], rt[:, 0:1], None, ALU.max),
                          waits=[("S", sab)])
                      P.op("vector", lambda h, pg=pg: h.tensor_scalar(
                          r8[:, 24:32], pg[:, 0:8], rt[:, 0:1], None, ALU.subtract))
                      P.op("vector", lambda h: h.scalar_tensor_tensor(
                          r8[:, 32:40], r8[:, 16:24], JIT2, r8[:, 24:32], ALU.mult, ALU.add))
                      P.op("vector", lambda h: h.tensor_scalar(
                          r8[:, 32:40], r8[:, 32:40], 0.0, None, ALU.is_ge))
                      vm = P.op("vector", lambda h: h.tensor_scalar(
                          rt[:, 1:2], rt[:, 0:1], -1.0, None, ALU.mult), inc="V")
                      se = P.op("scalar", lambda h, pg=pg: h.activation(
                          r8[:, 40:48], pg[:, 0:8], AF.Exp, bias=rt[:, 1:2]),
                          waits=[("V", vm)], inc="S")
                      P.op("vector", lambda h: h.tensor_mul(
                          r8[:, 40:48], r8[:, 40:48], r8[:, 32:40]), waits=[("S", se)])
                      P.op("vector", lambda h: h.reduce_sum(rt[:, 4:5], r8[:, 40:48], axis=AX.X))
                      P.op("vector", lambda h: h.reciprocal(rt[:, 6:7], rt[:, 4:5]))
                      P.op("vector", lambda h, pg=pg: h.scalar_tensor_tensor(
                          r8[:, 56:64], r8[:, 0:8], NEG_BIG, pg[:, 0:8], ALU.mult, ALU.add))
                      P.op("vector", lambda h: h.reduce_max(rt[:, 2:3], r8[:, 56:64], axis=AX.X))
                      P.op("vector", lambda h: h.tensor_scalar(
                          r8[:, 48:56], r8[:, 56:64], rt[:, 2:3], None, ALU.is_equal))
                      P.op("vector", lambda h: h.tensor_scalar(
                          r8[:, 16:24], r8[:, 8:16], rt[:, 2:3], None, ALU.max))
                      P.op("vector", lambda h, pg=pg: h.tensor_scalar(
                          r8[:, 24:32], pg[:, 0:8], rt[:, 2:3], None, ALU.subtract))
                      P.op("vector", lambda h: h.scalar_tensor_tensor(
                          r8[:, 24:32], r8[:, 16:24], JIT2, r8[:, 24:32], ALU.mult, ALU.add))
                      P.op("vector", lambda h: h.tensor_scalar(
                          r8[:, 24:32], r8[:, 24:32], 0.0, None, ALU.is_ge))
                      P.op("vector", lambda h: h.tensor_sub(
                          r8[:, 24:32], r8[:, 24:32], r8[:, 0:8]))
                      vm2 = P.op("vector", lambda h: h.tensor_scalar(
                          rt[:, 3:4], rt[:, 2:3], -1.0, None, ALU.mult), inc="V")
                      se2 = P.op("scalar", lambda h, pg=pg: h.activation(
                          r8[:, 64:72], pg[:, 0:8], AF.Exp, bias=rt[:, 3:4]),
                          waits=[("V", vm2)], inc="S")
                      s_e2g[m] = se2
                      P.op("vector", lambda h: h.tensor_mul(
                          r8[:, 64:72], r8[:, 64:72], r8[:, 24:32]), waits=[("S", se2)])
                      P.op("vector", lambda h: h.reduce_sum(rt[:, 5:6], r8[:, 64:72], axis=AX.X))
                      P.op("vector", lambda h: h.reciprocal(rt[:, 7:8], rt[:, 5:6]))
                      P.op("vector", lambda h: h.tensor_scalar(
                          r8[:, 48:56], r8[:, 48:56], rt[:, 7:8], None, ALU.mult))
                      v_wtok[m] = P.op("vector", lambda h, m=m: h.scalar_tensor_tensor(
                          wtok_sb[:, m * 8:(m + 1) * 8], r8[:, 0:8], rt[:, 6:7],
                          r8[:, 48:56], ALU.mult, ALU.add), inc="V")

                  qkv_part("q", xnT_o, 16, wqb, g_wq, tl_o, d_q_s, d_cos_o, d_sin_o, True, True)
                  P.barrier()
                  P.flush()

              with contextlib.ExitStack() as pk_scope:
                  KE = pk_scope.enter_context
                  xnT_c = [KE(nc.sbuf_tensor(f"xnT_c{k}", [128, T], BF16,
                                             is_transpose_dest=True)) for k in range(8)]
                  wkb = KE(nc.sbuf_tensor("wkb", [128, 8 * C], BF16))
                  wvb = KE(nc.sbuf_tensor("wvb", [128, 8 * C], BF16))
                  tl_c = None
                  for k in range(8):
                      tl_c = P.op("sync", lambda h, k=k: h.dma_start_transpose(
                          xnT_c[k][:, :], d_xn_c[:, k * 128:(k + 1) * 128]),
                          inc="dT", inc_n=16)
                  g_wk = None
                  for k in range(8):
                      g_wk = P.op("gpsimd", lambda h, k=k: h.dma_start(
                          wkb[:, k * C:(k + 1) * C], d_wk[k * 128:(k + 1) * 128, :]),
                          inc="dG", inc_n=16)
                  g_wv = None
                  for k in range(8):
                      g_wv = P.op("gpsimd", lambda h, k=k: h.dma_start(
                          wvb[:, k * C:(k + 1) * C], d_wv[k * 128:(k + 1) * 128, :]),
                          inc="dG", inc_n=16)
                  g_wk = g_wv
                  qkv_part("k", xnT_c, 32, wkb, g_wk, tl_c, d_k_s, d_cos_c, d_sin_c, True, False)
                  qkv_part("v", xnT_c, 32, wvb, g_wv, tl_c, d_v_s, None, None, False, False)
                  P.barrier()
                  P.flush()

        with contextlib.ExitStack() as ph:
          if phases >= 3:
              PE = ph.enter_context
              xnT_o = [PE(nc.sbuf_tensor(f"f_xnT{k}", [128, TOK], BF16,
                                         is_transpose_dest=True)) for k in range(8)]
              w1c = [PE(nc.sbuf_tensor(f"w1c{j}", [128, 8192], BF16)) for j in range(2)]
              w3c = [PE(nc.sbuf_tensor(f"w3c{j}", [128, 8192], BF16)) for j in range(2)]
              w2c = [PE(nc.sbuf_tensor(f"w2c{j}", [128, 8192], BF16)) for j in range(2)]
              sbuf_s = [PE(nc.sbuf_tensor(f"f_s{j}", [128, TBS], BF16)) for j in range(2)]
              hbuf = [PE(nc.sbuf_tensor(f"f_h{j}", [128, TBS], BF16)) for j in range(2)]
              ups = [PE(nc.psum_tensor(f"ups{j}", [128, 512], F32)) for j in range(2)]
              wps = [PE(nc.psum_tensor(f"wps{j}", [128, 512], F32)) for j in range(2)]
              yps = PE(nc.psum_tensor("yps", [128, 2048], F32))

              tl = None
              for k in range(8):
                  tl = P.op("sync", lambda h, k=k: h.dma_start_transpose(
                      xnT_o[k][:, :], d_xn_o[:, k * 128:(k + 1) * 128]), inc="dT", inc_n=16)
              P.op("vector", lambda h: h.memset(y_sb[:, :], 0.0), inc="V")

              groups = [(e, fg) for e in range(NEXP) for fg in range(NGRP)]
              pe_end, dg_end, v_ystt, t_mm2 = {}, {}, {}, {}
              u_guard, w_guard, s_guard, h_guard = {}, {}, {}, {}
              for g, (e, fg) in enumerate(groups):
                  buf = g % 2
                  gg = [("T", pe_end[g - 2])] if g - 2 in pe_end else []
                  gsem = f"dG{g % 2}"
                  for k in range(8):
                      P.op("gpsimd", lambda h, e=e, fg=fg, k=k, buf=buf: h.dma_start(
                          w1c[buf][:, k * 1024:(k + 1) * 1024],
                          d_w1[e, k * 128:(k + 1) * 128, fg * FG:(fg + 1) * FG]),
                          waits=gg if k == 0 else [], inc=gsem, inc_n=16)
                  for k in range(8):
                      P.op("gpsimd", lambda h, e=e, fg=fg, k=k, buf=buf: h.dma_start(
                          w3c[buf][:, k * 1024:(k + 1) * 1024],
                          d_w3[e, k * 128:(k + 1) * 128, fg * FG:(fg + 1) * FG]),
                          inc=gsem, inc_n=16)
                  for ft in range(8):
                      dg_end[g] = P.op("gpsimd", lambda h, e=e, fg=fg, ft=ft, buf=buf: h.dma_start(
                          w2c[buf][:, ft * 1024:(ft + 1) * 1024],
                          d_w2[e, fg * FG + ft * 128: fg * FG + (ft + 1) * 128, :]),
                          inc=gsem, inc_n=16)

                  for tb in range(NTB):
                      t_mm1, t_mm3, s_silu, v_h = {}, {}, {}, {}

                      def emit_mm2(fm, g=g, tb=tb, buf=buf, v_h=None):
                          sl = fm % 2
                          w = [("V", v_h[fm])]
                          if fm == 0:
                              if (g, tb - 1) in v_ystt:
                                  w.append(("V", v_ystt[(g, tb - 1)]))
                              elif (g - 1, NTB - 1) in v_ystt:
                                  w.append(("V", v_ystt[(g - 1, NTB - 1)]))
                          last = None
                          for half in range(2):
                              for n2 in range(2):
                                  last = P.op("tensor",
                                              lambda h, sl=sl, half=half, n2=n2, fm=fm, buf=buf: h.matmul(
                                      yps[:, half * 1024 + n2 * 512: half * 1024 + n2 * 512 + 512],
                                      hbuf[sl][:, half * 128:(half + 1) * 128],
                                      w2c[buf][:, fm * 1024 + n2 * 512: fm * 1024 + n2 * 512 + 512],
                                      start=(fm == 0), stop=(fm == 7)),
                                      waits=w if (half == 0 and n2 == 0) else [],
                                      inc="T")
                          t_mm2[(g, tb, fm)] = last
                          h_guard[sl] = ("T", last)

                      for fm in range(8):
                          sl = fm % 2
                          w1l = [(f"dG{g % 2}", dg_end[g])] if (tb == 0 and fm == 0) else []
                          if sl in u_guard:
                              w1l.append(u_guard[sl])
                          if tb == 0 and fm == 0 and g == 0:
                              w1l.append(("dT", tl))
                          tm1 = None
                          for k in range(8):
                              r = P.op("tensor",
                                       lambda h, k=k, fm=fm, tb=tb, buf=buf, sl=sl: h.matmul(
                                  ups[sl][:, 0:TBS],
                                  w1c[buf][:, k * 1024 + fm * 128: k * 1024 + (fm + 1) * 128],
                                  xnT_o[k][:, tb * TBS:(tb + 1) * TBS],
                                  start=(k == 0), stop=(k == 7)),
                                  waits=w1l if k == 0 else [], inc="T" if k == 7 else None)
                              if r is not None:
                                  tm1 = r
                          t_mm1[fm] = tm1
                          w3l = [w_guard[sl]] if sl in w_guard else []
                          tm3 = None
                          for k in range(8):
                              r = P.op("tensor",
                                       lambda h, k=k, fm=fm, tb=tb, buf=buf, sl=sl: h.matmul(
                                  wps[sl][:, 0:TBS],
                                  w3c[buf][:, k * 1024 + fm * 128: k * 1024 + (fm + 1) * 128],
                                  xnT_o[k][:, tb * TBS:(tb + 1) * TBS],
                                  start=(k == 0), stop=(k == 7)),
                                  waits=w3l if k == 0 else [], inc="T" if k == 7 else None)
                              if r is not None:
                                  tm3 = r
                          t_mm3[fm] = tm3
                          if fm > 0:
                              emit_mm2(fm - 1, v_h=v_h)
                          sg = [s_guard[sl]] if sl in s_guard else []
                          s_silu[fm] = P.op("scalar", lambda h, sl=sl: h.activation(
                              sbuf_s[sl][:, :], ups[sl][:, 0:TBS], AF.Silu),
                              waits=[("T", t_mm1[fm])] + sg, inc="S")
                          u_guard[sl] = ("S", s_silu[fm])
                          hg = [h_guard[sl]] if sl in h_guard else []
                          v_h[fm] = P.op("vector", lambda h, sl=sl: h.tensor_mul(
                              hbuf[sl][:, :], sbuf_s[sl][:, :], wps[sl][:, 0:TBS]),
                              waits=[("S", s_silu[fm]), ("T", t_mm3[fm])] + hg, inc="V")
                          w_guard[sl] = ("V", v_h[fm])
                          s_guard[sl] = ("V", v_h[fm])
                      emit_mm2(7, v_h=v_h)
                      tile0 = tb * 2
                      for half in range(2):
                          v_ystt[(g, tb)] = P.op("vector",
                                                 lambda h, half=half, tile0=tile0, e=e: h.scalar_tensor_tensor(
                              y_sb[:, (tile0 + half) * 1024:(tile0 + half + 1) * 1024],
                              yps[:, half * 1024:(half + 1) * 1024],
                              wtok_sb[:, (tile0 + half) * 8 + e: (tile0 + half) * 8 + e + 1],
                              y_sb[:, (tile0 + half) * 1024:(tile0 + half + 1) * 1024],
                              ALU.mult, ALU.add),
                              waits=[("T", t_mm2[(g, tb, 7)])], inc="V")
                  pe_end[g] = t_mm2[(g, NTB - 1, 7)]
              P.barrier()
              P.flush()

        with contextlib.ExitStack() as ph:
          if phases >= 4:
              PE = ph.enter_context
              qt = [PE(nc.sbuf_tensor(f"a_qt{j}", [128, TOK], BF16, is_transpose_dest=True))
                    for j in range(2)]
              kt = [PE(nc.sbuf_tensor(f"a_kt{j}", [128, T], BF16, is_transpose_dest=True))
                    for j in range(2)]
              vh = [PE(nc.sbuf_tensor(f"a_vh{j}", [128, T], BF16)) for j in range(2)]
              e0b = [PE(nc.sbuf_tensor(f"a_e0{j}", [128, T], BF16)) for j in range(2)]
              e1b = [PE(nc.sbuf_tensor(f"a_e1{j}", [128, T], BF16)) for j in range(2)]
              ptrb = PE(nc.sbuf_tensor("a_ptr", [128, 8192], BF16))
              eraw = [PE(nc.sbuf_tensor(f"a_eraw{j}", [128, 512], BF16)) for j in range(2)]
              maskb = PE(nc.sbuf_tensor("a_mask", [128, 1024], BF16))
              zc = PE(nc.sbuf_tensor("a_zc", [128, 48], F32))
              avsq = PE(nc.sbuf_tensor("a_avsq", [128, 128], F32))
              avsb = [PE(nc.sbuf_tensor(f"a_avsb{j}", [128, 128], F32)) for j in range(2)]
              avnb = [PE(nc.sbuf_tensor(f"a_avn{j}", [128, 128], BF16)) for j in range(2)]
              s0ps = PE(nc.psum_tensor("s0ps", [128, 1024], F32))
              s1ps = PE(nc.psum_tensor("s1ps", [128, 1024], F32))
              ptrps = [PE(nc.psum_tensor(f"ptrps{j}", [128, 1024], BF16)) for j in range(2)]
              avps = [PE(nc.psum_tensor(f"avps{j}", [128, 512], F32)) for j in range(2)]

              gm = P.op("gpsimd", lambda h: h.dma_start(maskb[:, :], d_mask),
                        inc="dG", inc_n=16)
              P.op("vector", lambda h: h.memset(zc[:, :], 0.0), inc="V")
              t_avend = {}
              sps_guard = {}
              av_guard = {}
              avsb_g = {}
              avn_dg = {}

              def emit_loads(hp, ctx):
                  hb = hp % 2
                  lg = [("T", t_avend[hp - 2])] if hp - 2 in t_avend else []
                  asem = f"dA{hp % 2}"
                  P.op("sync", lambda h, hp=hp, hb=hb: h.dma_start_transpose(
                      qt[hb][:, :], d_q_s[:, hp * 128:(hp + 1) * 128]), waits=lg,
                      inc=asem, inc_n=16)
                  P.op("sync", lambda h, hp=hp, hb=hb: h.dma_start_transpose(
                      kt[hb][:, :], d_k_s[:, hp * 128:(hp + 1) * 128]), inc=asem, inc_n=16)
                  ld_h = None
                  for kb4 in range(8):
                      ld_h = P.op("sync", lambda h, hp=hp, hb=hb, kb4=kb4: h.dma_start(
                          vh[hb][:, kb4 * 512:(kb4 + 1) * 512].rearrange(
                              "p (kb d) -> p kb d", d=128),
                          d_v_s[kb4 * 512:(kb4 + 1) * 512,
                                hp * 128:(hp + 1) * 128].rearrange(
                              "(kb p) d -> p kb d", p=128)),
                          inc=asem, inc_n=16)
                  ctx[("ld", hp)] = ld_h

              def emit_front(hp, i, ctx):
                  hb = hp % 2
                  asem = f"dA{hp % 2}"
                  kcn = i // 2 + 1
                  eslot = i % 2
                  zb = 24 * eslot
                  s_e, v_e = {}, {}
                  nfull = kcn - 1
                  npairs = (nfull + 1) // 2
                  for j in range((nfull + 1) // 2):
                      k0 = 2 * j
                      wide = min(2, nfull - k0)
                      gq = [(asem, ctx[("ld", hp)])] if (i <= 1 and j == 0) else []
                      gps_ = [sps_guard[0]] if 0 in sps_guard else []
                      eg = []
                      if j == 0 and ("tr_done", hp, i - 2) in ctx:
                          eg = [("T", ctx[("tr_done", hp, i - 2)])]
                      elif j == 0 and i < 2 and ("tr_last", hp - 1) in ctx:
                          eg = [("T", ctx[("tr_last", hp - 1)])]
                      t_s0 = t_s1 = None
                      for c2_ in range(wide):
                          kc = k0 + c2_
                          t_s0 = P.op("tensor", lambda h, hb=hb, i=i, kc=kc, c2_=c2_: h.matmul(
                              s0ps[:, c2_ * 512:(c2_ + 1) * 512],
                              qt[hb][0:64, i * 128:(i + 1) * 128],
                              kt[hb][0:64, kc * 512:(kc + 1) * 512], start=True, stop=True),
                              waits=(gq + gps_ + eg) if c2_ == 0 else [], inc="T")
                          t_s1 = P.op("tensor", lambda h, hb=hb, i=i, kc=kc, c2_=c2_: h.matmul(
                              s1ps[:, c2_ * 512:(c2_ + 1) * 512],
                              qt[hb][64:128, i * 128:(i + 1) * 128],
                              kt[hb][64:128, kc * 512:(kc + 1) * 512], start=True, stop=True),
                              inc="T")
                      s_e[(0, j)] = P.op("scalar", lambda h, k0=k0, wide=wide, eslot=eslot, zb=zb: h.activation(
                          e0b[eslot][:, k0 * 512:(k0 + wide) * 512],
                          s0ps[:, 0:wide * 512], AF.Exp,
                          accum_out=zc[:, zb + k0 // 2:zb + k0 // 2 + 1]),
                          waits=[("T", t_s0)], inc="S")
                      s_e[(1, j)] = P.op("scalar", lambda h, k0=k0, wide=wide, eslot=eslot, zb=zb: h.activation(
                          e1b[eslot][:, k0 * 512:(k0 + wide) * 512],
                          s1ps[:, 0:wide * 512], AF.Exp,
                          accum_out=zc[:, zb + 8 + k0 // 2:zb + 9 + k0 // 2]),
                          waits=[("T", t_s1)], inc="S")
                      sps_guard[0] = ("S", s_e[(1, j)])
                  kc = kcn - 1
                  par = i % 2
                  gq = [(asem, ctx[("ld", hp)])] if (i <= 1 and kc == 0) else []
                  gps_ = [sps_guard[0]] if 0 in sps_guard else []
                  eg = []
                  if kc == 0 and ("tr_done", hp, i - 2) in ctx:
                      eg = [("T", ctx[("tr_done", hp, i - 2)])]
                  elif kc == 0 and i < 2 and ("tr_last", hp - 1) in ctx:
                      eg = [("T", ctx[("tr_last", hp - 1)])]
                  t_s0 = P.op("tensor", lambda h, hb=hb, i=i, kc=kc: h.matmul(
                      s0ps[:, 0:512], qt[hb][0:64, i * 128:(i + 1) * 128],
                      kt[hb][0:64, kc * 512:(kc + 1) * 512], start=True, stop=True),
                      waits=gq + gps_ + eg, inc="T")
                  t_s1 = P.op("tensor", lambda h, hb=hb, i=i, kc=kc: h.matmul(
                      s1ps[:, 512:1024], qt[hb][64:128, i * 128:(i + 1) * 128],
                      kt[hb][64:128, kc * 512:(kc + 1) * 512], start=True, stop=True),
                      inc="T")
                  erg = []
                  if ("v_eraw", hp, i - 2) in ctx:
                      erg = [("V", ctx[("v_eraw", hp, i - 2)])]
                  elif i < 2 and ("v_eraw_last", hp - 1) in ctx:
                      erg = [("V", ctx[("v_eraw_last", hp - 1)])]
                  sr0 = P.op("scalar", lambda h, eslot=eslot: h.activation(
                      eraw[eslot][:, :], s0ps[:, 0:512], AF.Exp),
                      waits=[("T", t_s0)] + erg, inc="S")
                  v_e[(0, kc)] = P.op("vector", lambda h, kc=kc, par=par, eslot=eslot, zb=zb: h.scalar_tensor_tensor(
                      e0b[eslot][:, kc * 512:(kc + 1) * 512], eraw[eslot][:, :], 1.0,
                      maskb[:, par * 512:(par + 1) * 512], ALU.mult, ALU.mult,
                      accum_out=zc[:, zb + npairs:zb + npairs + 1]),
                      waits=[("S", sr0), ("dG", gm)], inc="V")
                  sr1 = P.op("scalar", lambda h, eslot=eslot: h.activation(
                      eraw[eslot][:, :], s1ps[:, 512:1024], AF.Exp),
                      waits=[("T", t_s1), ("V", v_e[(0, kc)])], inc="S")
                  v_e[(1, kc)] = P.op("vector", lambda h, kc=kc, par=par, eslot=eslot, zb=zb: h.scalar_tensor_tensor(
                      e1b[eslot][:, kc * 512:(kc + 1) * 512], eraw[eslot][:, :], 1.0,
                      maskb[:, par * 512:(par + 1) * 512], ALU.mult, ALU.mult,
                      accum_out=zc[:, zb + 8 + npairs:zb + 9 + npairs]),
                      waits=[("S", sr1)], inc="V")
                  ctx[("v_eraw", hp, i)] = v_e[(1, kc)]
                  ctx[("v_eraw_last", hp)] = v_e[(1, kc)]
                  sps_guard[0] = ("S", sr1)
                  wz = [("S", max(s_e.values()))] if s_e else []
                  if v_e:
                      wz.append(("V", v_e[(1, kcn - 1)]))
                  ctx[("exps_w", hp, i)] = list(wz)
                  P.op("vector", lambda h, zb=zb, npairs=npairs: h.reduce_sum(
                      zc[:, zb + 16:zb + 17], zc[:, zb:zb + npairs + 1], axis=AX.X), waits=wz)
                  P.op("vector", lambda h, zb=zb: h.reciprocal(
                      zc[:, zb + 16:zb + 17], zc[:, zb + 16:zb + 17]))
                  P.op("vector", lambda h, zb=zb, npairs=npairs: h.reduce_sum(
                      zc[:, zb + 17:zb + 18], zc[:, zb + 8:zb + 9 + npairs], axis=AX.X))
                  P.op("vector", lambda h, zb=zb: h.reciprocal(
                      zc[:, zb + 17:zb + 18], zc[:, zb + 17:zb + 18]))
                  vz = P.op("vector", lambda h, zb=zb: h.tensor_mul(
                      zc[:, zb + 17:zb + 18], zc[:, zb + 17:zb + 18], lamneg[:, 0:1]), inc="V")
                  ctx[("v_z", hp, i)] = vz

              def emit_back_pe(hp, i, ctx):
                  hb = hp % 2
                  kcn = i // 2 + 1
                  ext = 512 * kcn
                  nkb = ext // 128
                  eslot = i % 2
                  ngr = (nkb + 7) // 8
                  exps_done = ctx[("exps_w", hp, i)]
                  v_evac = {}
                  t_av = {}
                  groups = [(side, gr) for side in range(2) for gr in range(ngr)]
                  t_tr_last = None
                  for gidx, (side, gr) in enumerate(groups):
                      eb = e0b[eslot] if side == 0 else e1b[eslot]
                      kbs = list(range(gr * 8, min((gr + 1) * 8, nkb)))
                      pg_ = [("V", v_evac[gidx - 2])] if gidx - 2 in v_evac else []
                      aw = [("T", ctx[("t_avslot", gidx)])] if ("t_avslot", gidx) in ctx else []
                      t_tr = None
                      for kb in kbs:
                          r = P.op("tensor", lambda h, kb=kb, gidx=gidx, eb=eb: h.transpose(
                              ptrps[gidx % 2][:, (kb % 8) * 128:((kb % 8) + 1) * 128],
                              eb[:, kb * 128:(kb + 1) * 128], identb[:, :]),
                              waits=(exps_done + pg_ + aw + [("dG", g_id)]) if kb == kbs[0] else [],
                              inc="T" if kb == kbs[-1] else None)
                          if r is not None:
                              t_tr = r
                      t_tr_last = t_tr
                      wdt = len(kbs) * 128
                      v_evac[gidx] = P.op("vector", lambda h, gidx=gidx, wdt=wdt: h.tensor_copy(
                          ptrb[:, gidx * 1024: gidx * 1024 + wdt],
                          ptrps[gidx % 2][:, 0:wdt]), waits=[("T", t_tr)], inc="V")
                  for gidx, (side, gr) in enumerate(groups):
                      kbs = list(range(gr * 8, min((gr + 1) * 8, nkb)))
                      ag = av_guard.get(side)
                      last = None
                      for kb in kbs:
                          last = P.op("tensor", lambda h, kb=kb, hb=hb, side=side, gidx=gidx, nkb=nkb: h.matmul(
                              avps[side][:, 0:128],
                              ptrb[:, gidx * 1024 + (kb % 8) * 128: gidx * 1024 + (kb % 8 + 1) * 128],
                              vh[hb][:, kb * 128:(kb + 1) * 128],
                              start=(kb == 0), stop=(kb == nkb - 1)),
                              waits=([("V", v_evac[gidx])] + ([ag] if (ag and kb == 0) else [])) if kb == kbs[0] else [],
                              inc="T" if kb == kbs[-1] else None)
                      ctx[("t_avslot", gidx)] = last
                      t_av[side] = last
                  ctx[("t_av", hp, i)] = (t_av[0], t_av[1])
                  ctx[("tr_done", hp, i)] = t_tr_last
                  ctx[("tr_last", hp)] = t_tr_last

              def emit_back_post(hp, i, ctx):
                  ab_ = i % 2
                  zb = 24 * ab_
                  ta0, ta1 = ctx[("t_av", hp, i)]
                  ag2 = [avsb_g[ab_]] if ab_ in avsb_g else []
                  c1 = P.op("vector", lambda h, ab_=ab_, zb=zb: h.tensor_scalar(
                      avsb[ab_][:, :], avps[0][:, 0:128], zc[:, zb + 16:zb + 17],
                      None, ALU.mult), waits=[("T", ta0)] + ag2, inc="V")
                  c2 = P.op("vector", lambda h, ab_=ab_, zb=zb: h.scalar_tensor_tensor(
                      avsb[ab_][:, :], avps[1][:, 0:128], zc[:, zb + 17:zb + 18],
                      avsb[ab_][:, :], ALU.mult, ALU.add), waits=[("T", ta1)], inc="V")
                  av_guard[0] = ("V", c1)
                  av_guard[1] = ("V", c2)
                  P.op("vector", lambda h, ab_=ab_: h.reduce_sum(
                      zc[:, 44:45], avsb[ab_][:, :], axis=AX.X))
                  sq_ = P.op("scalar", lambda h, ab_=ab_: h.activation(
                      avsq[:, :], avsb[ab_][:, :], AF.Square,
                      accum_out=zc[:, 45:46]), waits=[("V", c2)], inc="S")
                  P.op("vector", lambda h: h.tensor_scalar(
                      zc[:, 44:45], zc[:, 44:45], 1.0 / 128, None, ALU.mult))
                  P.op("vector", lambda h: h.tensor_scalar(
                      zc[:, 45:46], zc[:, 45:46], 1.0 / 128, None, ALU.mult),
                      waits=[("S", sq_)])
                  P.op("vector", lambda h: h.tensor_scalar(
                      zc[:, 46:47], zc[:, 44:45], zc[:, 44:45], None, ALU.mult))
                  vv_ = P.op("vector", lambda h: h.tensor_sub(
                      zc[:, 45:46], zc[:, 45:46], zc[:, 46:47]), inc="V")
                  P.op("scalar", lambda h: h.activation(
                      zc[:, 45:46], zc[:, 45:46], AF.Ln, bias=constb[:, 0:1]),
                      waits=[("V", vv_)], inc="S")
                  sd_ = P.op("scalar", lambda h: h.activation(
                      zc[:, 45:46], zc[:, 45:46], AF.Exp, scale=constb[:, 3:4]), inc="S")
                  P.op("vector", lambda h: h.tensor_scalar(
                      zc[:, 45:46], zc[:, 45:46], 1.0 - LAM_INIT, None, ALU.mult),
                      waits=[("S", sd_)])
                  P.op("vector", lambda h: h.tensor_scalar(
                      zc[:, 46:47], zc[:, 45:46], -1.0, None, ALU.mult))
                  P.op("vector", lambda h: h.tensor_scalar(
                      zc[:, 46:47], zc[:, 44:45], zc[:, 46:47], None, ALU.mult))
                  ag3 = [(f"dN{i % 2}", avn_dg[i % 2])] if i % 2 in avn_dg else []
                  s_avn = P.op("vector", lambda h, ab_=ab_, i=i: h.tensor_scalar(
                      avnb[i % 2][:, :], avsb[ab_][:, :], zc[:, 45:46],
                      zc[:, 46:47], ALU.mult, ALU.add),
                      waits=ag3, inc="V")
                  avsb_g[ab_] = ("V", s_avn)
                  avn_dg[i % 2] = P.op("sync", lambda h, i=i, hp=hp: h.dma_start(
                      d_avn_s[i * 128:(i + 1) * 128, hp * 128:(hp + 1) * 128],
                      avnb[i % 2][:, :]), waits=[("V", s_avn)], inc=f"dN{i % 2}", inc_n=16)

              ctx = {}
              emit_loads(0, ctx)
              emit_loads(1, ctx)
              for hp in range(NHP):
                  for i in range(16):
                      if i >= 2:
                          emit_back_post(hp, i - 2, ctx)
                      emit_front(hp, i, ctx)
                      if i >= 1:
                          emit_back_pe(hp, i - 1, ctx)
                  emit_back_post(hp, 14, ctx)
                  emit_back_pe(hp, 15, ctx)
                  t_avend[hp] = P.cnt["T"]
                  if hp + 2 < NHP:
                      emit_loads(hp + 2, ctx)
                  emit_back_post(hp, 15, ctx)
              P.barrier()
              P.flush()


        with contextlib.ExitStack() as ph:
          if phases >= 5:
              PE = ph.enter_context
              avnT = [PE(nc.sbuf_tensor(f"pr_avnT{h2}", [128, TOK], BF16,
                                        is_transpose_dest=True)) for h2 in range(8)]
              wpb = PE(nc.sbuf_tensor("wpb", [128, 8 * C], BF16))
              xb2 = [PE(nc.sbuf_tensor(f"pr_x{j}", [128, C], F32)) for j in range(2)]
              gb = [PE(nc.sbuf_tensor(f"pr_g{j}", [128, C], F32)) for j in range(2)]
              ob = [PE(nc.sbuf_tensor(f"pr_o{j}", [128, C], F32)) for j in range(2)]
              tfb = [PE(nc.sbuf_tensor(f"pr_t{j}", [128, 512], F32)) for j in range(2)]
              prps = [PE(nc.psum_tensor(f"prps{j}", [128, 512], F32)) for j in range(2)]

              tla = None
              for h2 in range(8):
                  tla = P.op("sync", lambda h, h2=h2: h.dma_start_transpose(
                      avnT[h2][:, :], d_avn_s[:, h2 * 128:(h2 + 1) * 128]),
                      inc="dT", inc_n=16)
              g_wp = None
              for k in range(8):
                  g_wp = P.op("gpsimd", lambda h, k=k: h.dma_start(
                      wpb[:, k * C:(k + 1) * C], d_wp[k * 128:(k + 1) * 128, :]),
                      inc="dG", inc_n=16)
              v_out, st_out = {}, {}
              for m in range(16):
                  xg = [(f"dO{m % 2}", st_out[m - 2])] if m - 2 in st_out else []
                  lx = P.op("sync", lambda h, m=m: h.dma_start(
                      xb2[m % 2][:, :], d_x_own[m * 128:(m + 1) * 128, :]), waits=xg,
                      inc=f"dA{m % 2}", inc_n=16)
                  gg_ = [("V", v_out[(m - 2, 1)])] if (m - 2, 1) in v_out else []
                  sg_ = P.op("scalar", lambda h, m=m: h.activation(
                      gb[m % 2][:, :], y_sb[:, m * 1024:(m + 1) * 1024], AF.Silu),
                      waits=gg_, inc="S")
                  for n in range(2):
                      pbf = (2 * m + n) % 2
                      pg_ = [("V", v_out[(m - 1, n)])] if (m - 1, n) in v_out else []
                      t_pr = None
                      for h2 in range(8):
                          r = P.op("tensor", lambda h, h2=h2, m=m, n=n, pbf=pbf: h.matmul(
                              prps[pbf][:, :], avnT[h2][:, m * 128:(m + 1) * 128],
                              wpb[:, h2 * C + n * 512: h2 * C + n * 512 + 512],
                              start=(h2 == 0), stop=(h2 == 7)),
                              waits=([("dT", tla), ("dG", g_wp)] + pg_) if h2 == 0 else [],
                              inc="T" if h2 == 7 else None)
                          if r is not None:
                              t_pr = r
                      og = [(f"dO{m % 2}", st_out[m - 2])] if (n == 0 and m - 2 in st_out) else []
                      P.op("vector", lambda h, n=n, pbf=pbf, m=m: h.tensor_mul(
                          tfb[n][:, :], prps[pbf][:, :], gb[m % 2][:, n * 512:(n + 1) * 512]),
                          waits=[("T", t_pr), ("S", sg_)])
                      v_out[(m, n)] = P.op("vector", lambda h, m=m, n=n: h.tensor_add(
                          ob[m % 2][:, n * 512:(n + 1) * 512], tfb[n][:, :],
                          xb2[m % 2][:, n * 512:(n + 1) * 512]),
                          waits=[(f"dA{m % 2}", lx)] + og, inc="V")
                  st_out[m] = P.op("sync", lambda h, m=m: h.dma_start(
                      d_out[m * 128:(m + 1) * 128, :], ob[m % 2][:, :]),
                      waits=[("V", v_out[(m, 1)])], inc=f"dO{m % 2}", inc_n=16)

              if debug:
                  vd = P.op("vector", lambda h: h.tensor_copy(
                      tfb[0][:, 0:128], wtok_sb[:, :]), inc="V")
                  P.op("sync", lambda h: h.dma_start(dbg["dbg_wtok"], tfb[0][:, 0:128]),
                       waits=[("V", vd)], inc="dT", inc_n=16)
                  P.op("sync", lambda h: h.dma_start(dbg["dbg_y"], y_sb[:, :]),
                       inc="dT", inc_n=16)
                  for name, src in [("dbg_xn_own", d_xn_o), ("dbg_xn_ctx", d_xn_c),
                                    ("dbg_q", d_q_s), ("dbg_k", d_k_s), ("dbg_v", d_v_s),
                                    ("dbg_avn", d_avn_s)]:
                      P.op("sync", lambda h, name=name, src=src: h.dma_start(
                          dbg[name], src), inc="dT", inc_n=16)
              P.barrier()
              P.flush()

    return nc



def _tables(off):
    inv_freq = (10000.0 ** (-np.arange(0, HD, 2, dtype=np.float32) / HD)).astype(np.float32)

    def tab(pos):
        fr = np.outer(pos.astype(np.float32), inv_freq)
        c = np.cos(fr).astype(np.float32)
        s = np.sin(fr).astype(np.float32)
        cd = np.concatenate([c, c], axis=1)
        sd = np.concatenate([s, s], axis=1)
        return (np.ascontiguousarray(np.tile(cd, (1, 8))),
                np.ascontiguousarray(np.tile(sd, (1, 8))))

    c_o, s_o = tab(np.arange(off, 2 * TOK, 2))
    c_c, s_c = tab(np.arange(T))
    return c_o, s_o, c_c, s_c


def _mask(off):
    r = np.arange(128)[:, None]
    j = np.arange(512)[None, :]
    m = np.zeros((128, 1024), np.float32)
    m[:, 0:512] = (j <= 2 * r + off)
    m[:, 512:1024] = (j <= 256 + 2 * r + off)
    return m.astype(ml_dtypes.bfloat16)


_NC_CACHE = {}


def kernel(L0_attn_x, L0_attn_Wq, L0_attn_Wk, L0_attn_Wv, L0_attn_Wproj,
           L0_attn_lq1, L0_attn_lk1, L0_attn_lq2, L0_attn_lk2,
           L0_ffn_gate_w, L0_ffn_w1, L0_ffn_w2, L0_ffn_w3, _debug=False):
    f32 = np.float32
    x = np.ascontiguousarray(np.asarray(L0_attn_x, dtype=f32))
    wq_t = np.ascontiguousarray(np.asarray(L0_attn_Wq, dtype=f32).T)
    wk_t = np.ascontiguousarray(np.asarray(L0_attn_Wk, dtype=f32).T)
    wv_t = np.ascontiguousarray(np.asarray(L0_attn_Wv, dtype=f32).T)
    wp_t = np.ascontiguousarray(np.asarray(L0_attn_Wproj, dtype=f32).T)
    gate_t = np.ascontiguousarray(np.asarray(L0_ffn_gate_w, dtype=f32).T)
    w1_t = np.ascontiguousarray(np.asarray(L0_ffn_w1, dtype=f32).transpose(0, 2, 1))
    w3_t = np.ascontiguousarray(np.asarray(L0_ffn_w3, dtype=f32).transpose(0, 2, 1))
    w2_t = np.ascontiguousarray(np.asarray(L0_ffn_w2, dtype=f32).transpose(0, 2, 1))
    lqk = np.ascontiguousarray(np.concatenate(
        [np.asarray(a, dtype=f32).reshape(-1) for a in
         (L0_attn_lq1, L0_attn_lk1, L0_attn_lq2, L0_attn_lk2)])[None, :])
    ident = np.eye(128, dtype=f32).astype(ml_dtypes.bfloat16)

    key = ("prog", _debug)
    if key not in _NC_CACHE:
        _NC_CACHE[key] = build_program(debug=_debug)
    nc = _NC_CACHE[key]

    tabs = {off: _tables(off) for off in (0, 1)}
    masks = {off: _mask(off) for off in (0, 1)}
    in_maps = []
    for c in range(8):
        b, off = c // 2, c % 2
        c_o, s_o, c_c, s_c = tabs[off]
        in_maps.append({
            "x_own": np.ascontiguousarray(x[b, off::2, :]),
            "x_ctx": np.ascontiguousarray(x[b]),
            "wq_t": wq_t, "wk_t": wk_t, "wv_t": wv_t, "wproj_t": wp_t,
            "gate_t": gate_t, "w1_t": w1_t, "w3_t": w3_t, "w2_t": w2_t,
            "lqk": lqk, "cos_own": c_o, "sin_own": s_o,
            "cos_ctx": c_c, "sin_ctx": s_c, "mask": masks[off], "ident": ident,
        })

    res = run_bass_kernel_spmd(nc, in_maps, core_ids=list(range(8)))
    out = np.empty((B, T, C), np.float32)
    for c in range(8):
        b, off = c // 2, c % 2
        out[b, off::2, :] = res.results[c]["out"]
    if _debug:
        return out, res
    return out


# version 08 (final)
# speedup vs baseline: 1.3644x; 1.1396x over previous
import contextlib
import math
import os

import numpy as np
import ml_dtypes

import concourse.bass as bass
import concourse.mybir as mybir
from concourse.bass_utils import run_bass_kernel_spmd

F32 = mybir.dt.float32
BF16 = mybir.dt.bfloat16
AF = mybir.ActivationFunctionType
ALU = mybir.AluOpType
AX = mybir.AxisListType

B, T, C = 4, 4096, 1024
TOK = 2048
NHP = 8
HD = 64
LAM_INIT = 0.8 - 0.6 * math.exp(-0.3 * 4)
RMS_EPS = float(np.finfo(np.float32).eps)
LN_EPS = 1e-5
NEXP, FF = 8, 4096
FG = 1024
NGRP = FF // FG
TBS = 256
NTB = TOK // TBS
JIT2 = 0.02
NEG_BIG = -1e30

ENGS = ("sync", "gpsimd", "tensor", "vector", "scalar")


class Prog:

    def __init__(self, blk, sems):
        self.blk = blk
        self.sems = sems
        self.cnt = {k: 0 for k in sems}
        self.waited = {}
        self.q = {e: [] for e in ENGS}

    SELF_SEM = {"vector": "V", "scalar": "S"}

    def op(self, eng, fn, waits=(), inc=None, inc_n=1):
        waits = [(s, v) for (s, v) in waits if v and v > 0]
        own = self.SELF_SEM.get(eng)
        if own is not None and fn is not None:
            if self.cnt[own] > 0:
                waits.append((own, self.cnt[own]))
            if inc is None:
                inc = own
        if inc is not None:
            self.cnt[inc] += inc_n
        self.q[eng].append((waits, fn, inc, inc_n))
        return self.cnt[inc] if inc is not None else None

    def barrier(self):
        vals = [(k, v) for k, v in self.cnt.items() if v > 0]
        for e in ENGS:
            self.op(e, None, waits=vals)

    def flush(self):
        for e in ENGS:
            ops = self.q[e]
            if not ops:
                continue
            sec = getattr(self.blk, e)

            def body(h, eng_name=e, ops_local=list(ops)):
                for waits, fn, inc, inc_n in ops_local:
                    for s, v in waits:
                        key = (eng_name, s)
                        if self.waited.get(key, 0) < v:
                            h.wait_ge(self.sems[s], v)
                            self.waited[key] = v
                    if fn is not None:
                        instr = fn(h)
                        if inc is not None:
                            instr.then_inc(self.sems[inc], inc_n)

            sec(body)
            self.q[e] = []


def build_program(debug=False, phases=5):
    nc = bass.Bass("TRN2", target_bir_lowering=False)

    d_x_own = nc.dram_tensor("x_own", [TOK, C], F32, kind="ExternalInput").ap()
    d_x_ctx = nc.dram_tensor("x_ctx", [T, C], F32, kind="ExternalInput").ap()
    d_wq = nc.dram_tensor("wq_t", [C, C], F32, kind="ExternalInput").ap()
    d_wk = nc.dram_tensor("wk_t", [C, C], F32, kind="ExternalInput").ap()
    d_wv = nc.dram_tensor("wv_t", [C, C], F32, kind="ExternalInput").ap()
    d_wp = nc.dram_tensor("wproj_t", [C, C], F32, kind="ExternalInput").ap()
    d_gate = nc.dram_tensor("gate_t", [C, NEXP], F32, kind="ExternalInput").ap()
    d_w1 = nc.dram_tensor("w1_t", [NEXP, C, FF], F32, kind="ExternalInput").ap()
    d_w3 = nc.dram_tensor("w3_t", [NEXP, C, FF], F32, kind="ExternalInput").ap()
    d_w2 = nc.dram_tensor("w2_t", [NEXP, FF, C], F32, kind="ExternalInput").ap()
    d_lqk = nc.dram_tensor("lqk", [1, 4 * HD], F32, kind="ExternalInput").ap()
    d_cos_o = nc.dram_tensor("cos_own", [TOK, 512], F32, kind="ExternalInput").ap()
    d_sin_o = nc.dram_tensor("sin_own", [TOK, 512], F32, kind="ExternalInput").ap()
    d_cos_c = nc.dram_tensor("cos_ctx", [T, 512], F32, kind="ExternalInput").ap()
    d_sin_c = nc.dram_tensor("sin_ctx", [T, 512], F32, kind="ExternalInput").ap()
    d_mask = nc.dram_tensor("mask", [128, 1024], BF16, kind="ExternalInput").ap()
    d_ident = nc.dram_tensor("ident", [128, 128], BF16, kind="ExternalInput").ap()

    d_out = nc.dram_tensor("out", [TOK, C], F32, kind="ExternalOutput").ap()

    d_xn_o = nc.dram_tensor("xn_own_s", [TOK, C], BF16, kind="Internal").ap()
    d_xn_c = nc.dram_tensor("xn_ctx_s", [T, C], BF16, kind="Internal").ap()
    d_q_s = nc.dram_tensor("q_s", [TOK, C], BF16, kind="Internal").ap()
    d_k_s = nc.dram_tensor("k_s", [T, C], BF16, kind="Internal").ap()
    d_v_s = nc.dram_tensor("v_s", [T, C], BF16, kind="Internal").ap()
    d_avn_s = nc.dram_tensor("avn_s", [TOK, C], BF16, kind="Internal").ap()

    dbg = {}
    if debug:
        for name, shp, dt in [
            ("dbg_xn_own", [TOK, C], BF16), ("dbg_xn_ctx", [T, C], BF16),
            ("dbg_q", [TOK, C], BF16), ("dbg_k", [T, C], BF16), ("dbg_v", [T, C], BF16),
            ("dbg_wtok", [128, 128], F32), ("dbg_y", [128, 16384], F32),
            ("dbg_avn", [TOK, C], BF16),
        ]:
            dbg[name] = nc.dram_tensor(name, shp, dt, kind="ExternalOutput").ap()

    with contextlib.ExitStack() as root:
        E = root.enter_context
        semnames = ("T", "V", "S", "dG", "dG0", "dG1", "dT",
                    "dX0", "dX1", "dX2", "dW0", "dW1",
                    "dTab0", "dTab1", "dQ0", "dQ1",
                    "dA0", "dA1", "dN0", "dN1", "dO0", "dO1")
        sems = {k: E(nc.semaphore(k)) for k in semnames}
        blk = E(nc.Block())
        P = Prog(blk, sems)

        wtok_sb = E(nc.sbuf_tensor("wtok_sb", [128, 128], F32))
        lamneg = E(nc.sbuf_tensor("lamneg", [128, 1], F32))
        y_sb = E(nc.sbuf_tensor("y_sb", [128, 16 * 1024], F32))
        identb = E(nc.sbuf_tensor("identb", [128, 128], BF16))
        constb = E(nc.sbuf_tensor("constb", [128, 4], F32))

        g_id = P.op("gpsimd", lambda h: h.dma_start(identb[:, :], d_ident),
                    inc="dG", inc_n=16)
        P.op("vector", lambda h: h.memset(constb[:, 0:1], LN_EPS))
        P.op("vector", lambda h: h.memset(constb[:, 1:2], RMS_EPS))
        P.op("vector", lambda h: h.memset(constb[:, 2:3], 1.0 / HD))
        P.op("vector", lambda h: h.memset(constb[:, 3:4], -0.5), inc="V")

        with contextlib.ExitStack() as ph:
          if phases >= 1:
              PE = ph.enter_context
              xb = [PE(nc.sbuf_tensor(f"p0_x{j}", [128, C], F32)) for j in range(3)]
              sqs = [PE(nc.sbuf_tensor(f"p0_sq{j}", [128, C], F32)) for j in range(2)]
              xnb = [PE(nc.sbuf_tensor(f"p0_xn{j}", [128, C], BF16)) for j in range(2)]
              st = [PE(nc.sbuf_tensor(f"p0_st{j}", [128, 8], F32)) for j in range(2)]

              tiles = [("own", t) for t in range(16)] + [("ctx", t) for t in range(32)]
              s_xn, sp_done, v_fin, ssq_d = {}, {}, {}, {}
              for idx, (kind, t) in enumerate(tiles):
                  src = d_x_own if kind == "own" else d_x_ctx
                  dst = d_xn_o if kind == "own" else d_xn_c
                  b3, b2 = idx % 3, idx % 2
                  x_t, sq_t, xn_t, s_t = xb[b3], sqs[b2], xnb[b2], st[b2]
                  w_guard = []
                  if idx - 3 in s_xn:
                      w_guard = [("V", s_xn[idx - 3]), ("S", ssq_d[idx - 3])]
                  xsem = f"dX{idx % 3}"
                  ld = P.op("sync", lambda h, a=x_t, s=src, t=t: h.dma_start(
                      a[:, :], s[t * 128:(t + 1) * 128, :]), waits=w_guard, inc=xsem, inc_n=16)
                  g2 = [("V", v_fin[idx - 2])] if idx - 2 in v_fin else []
                  P.op("vector", lambda h, a=s_t, x=x_t: h.reduce_sum(
                      a[:, 0:1], x[:, :], axis=AX.X), waits=[(xsem, ld)] + g2)
                  ssq = P.op("scalar", lambda h, a=sq_t, x=x_t, s=s_t: h.activation(
                      a[:, :], x[:, :], AF.Square, accum_out=s[:, 1:2]),
                      waits=[(xsem, ld)] + g2, inc="S")
                  ssq_d[idx] = ssq
                  P.op("vector", lambda h, s=s_t: h.tensor_scalar(
                      s[:, 2:3], s[:, 0:1], 1.0 / C, None, ALU.mult))
                  P.op("vector", lambda h, s=s_t: h.tensor_scalar(
                      s[:, 3:4], s[:, 1:2], 1.0 / C, None, ALU.mult), waits=[("S", ssq)])
                  P.op("vector", lambda h, s=s_t: h.tensor_scalar(
                      s[:, 4:5], s[:, 2:3], s[:, 2:3], None, ALU.mult))
                  vv = P.op("vector", lambda h, s=s_t: h.tensor_sub(
                      s[:, 5:6], s[:, 3:4], s[:, 4:5]), inc="V")
                  P.op("scalar", lambda h, s=s_t: h.activation(
                      s[:, 6:7], s[:, 5:6], AF.Ln, bias=constb[:, 0:1]),
                      waits=[("V", vv)], inc="S")
                  sd = P.op("scalar", lambda h, s=s_t: h.activation(
                      s[:, 6:7], s[:, 6:7], AF.Exp, scale=constb[:, 3:4]), inc="S")
                  P.op("vector", lambda h, s=s_t: h.tensor_scalar(
                      s[:, 7:8], s[:, 6:7], -1.0, None, ALU.mult), waits=[("S", sd)])
                  vn = P.op("vector", lambda h, s=s_t: h.tensor_scalar(
                      s[:, 7:8], s[:, 2:3], s[:, 7:8], None, ALU.mult), inc="V")
                  v_fin[idx] = vn
                  wsem = f"dW{idx % 2}"
                  xg = [(wsem, sp_done[idx - 2])] if idx - 2 in sp_done else []
                  sx = P.op("vector", lambda h, a=xn_t, x=x_t, s=s_t: h.tensor_scalar(
                      a[:, :], x[:, :], s[:, 6:7], s[:, 7:8], ALU.mult, ALU.add),
                      waits=xg, inc="V")
                  s_xn[idx] = sx
                  sp_done[idx] = P.op("sync", lambda h, a=xn_t, d=dst, t=t: h.dma_start(
                      d[t * 128:(t + 1) * 128, :], a[:, :]), waits=[("V", sx)],
                      inc=wsem, inc_n=16)
              P.barrier()
              P.flush()

        with contextlib.ExitStack() as p1:
          if phases >= 2:
              P1E = p1.enter_context
              qps = [P1E(nc.psum_tensor(f"qps{j}", [128, 512], F32)) for j in range(2)]
              ctab = [P1E(nc.sbuf_tensor(f"ctab{j}", [128, 512], F32)) for j in range(2)]
              stab = [P1E(nc.sbuf_tensor(f"stab{j}", [128, 512], F32)) for j in range(2)]
              sq5 = [P1E(nc.sbuf_tensor(f"sq5{j}", [128, 512], F32)) for j in range(2)]
              qn5 = [P1E(nc.sbuf_tensor(f"qn5{j}", [128, 512], F32)) for j in range(2)]
              ab5 = [P1E(nc.sbuf_tensor(f"ab5{j}", [128, 512], F32)) for j in range(2)]
              bb5 = [P1E(nc.sbuf_tensor(f"bb5{j}", [128, 512], F32)) for j in range(2)]
              s8 = [P1E(nc.sbuf_tensor(f"s8_{j}", [128, 16], F32)) for j in range(2)]
              qrb = [P1E(nc.sbuf_tensor(f"qrb{j}", [128, 512], BF16)) for j in range(2)]

              s_post, v_post, sp_q = {}, {}, {}

              def qkv_part(tag, xnT, n_m, wsb, gw, tlw, dsp, dcos, dsin, has_rope, is_q):
                  v_tab = {}
                  for m in range(n_m):
                      ct = None
                      tsem = f"dTab{m % 2}"
                      if has_rope:
                          tg2 = [("V", v_tab[m - 2])] if m - 2 in v_tab else []
                          P.op("sync", lambda h, m=m, dcos=dcos: h.dma_start(
                              ctab[m % 2][:, :], dcos[m * 128:(m + 1) * 128, :]),
                              waits=tg2, inc=tsem, inc_n=16)
                          ct = P.op("sync", lambda h, m=m, dsin=dsin: h.dma_start(
                              stab[m % 2][:, :], dsin[m * 128:(m + 1) * 128, :]),
                              inc=tsem, inc_n=16)
                      for n in range(2):
                          bi = (m * 2 + n) % 2
                          pq = qps[bi]
                          guard = []
                          if bi in s_post:
                              guard.append(("S", s_post[bi]))
                          if bi in v_post:
                              guard.append(("V", v_post[bi]))
                          tq = None
                          for k in range(8):
                              r = P.op("tensor", lambda h, k=k, m=m, n=n, pq=pq, xnT=xnT, wsb=wsb: h.matmul(
                                  pq[:, :], xnT[k][:, m * 128:(m + 1) * 128],
                                  wsb[:, k * C + n * 512: k * C + n * 512 + 512],
                                  start=(k == 0), stop=(k == 7)),
                                  waits=([("dG", gw), ("dT", tlw)] + guard) if k == 0 else [],
                                  inc="T" if k == 7 else None)
                              if r is not None:
                                  tq = r
                          if has_rope:
                              b2 = bi
                              ssq = P.op("scalar", lambda h, pq=pq, b2=b2: h.activation(
                                  sq5[b2][:, :], pq[:, :], AF.Square), waits=[("T", tq)], inc="S")
                              vred = None
                              for j in range(8):
                                  vred = P.op("vector", lambda h, j=j, b2=b2: h.reduce_sum(
                                      s8[b2][:, j:j + 1], sq5[b2][:, j * 64:(j + 1) * 64],
                                      axis=AX.X), waits=[("S", ssq)] if j == 0 else [],
                                      inc="V" if j == 7 else None)
                              P.op("scalar", lambda h, b2=b2: h.activation(
                                  s8[b2][:, 8:16], s8[b2][:, 0:8], AF.Ln,
                                  scale=constb[:, 2:3], bias=constb[:, 1:2]),
                                  waits=[("V", vred)], inc="S")
                              sd8 = P.op("scalar", lambda h, b2=b2: h.activation(
                                  s8[b2][:, 8:16], s8[b2][:, 8:16], AF.Exp,
                                  scale=constb[:, 3:4]), inc="S")
                              if is_q:
                                  P.op("vector", lambda h, b2=b2: h.tensor_scalar(
                                      s8[b2][:, 8:16], s8[b2][:, 8:16], 0.125, None, ALU.mult),
                                      waits=[("S", sd8)])
                              vpost = None
                              for j in range(8):
                                  vpost = P.op("vector", lambda h, j=j, pq=pq, b2=b2: h.tensor_scalar(
                                      qn5[b2][:, j * 64:(j + 1) * 64], pq[:, j * 64:(j + 1) * 64],
                                      s8[b2][:, 8 + j:9 + j], None, ALU.mult),
                                      waits=[("S", sd8)] if j == 0 else [],
                                      inc="V" if j == 7 else None)
                              v_post[b2] = vpost
                              s_post[b2] = ssq
                              P.op("vector", lambda h, b2=b2, m=m: h.tensor_mul(
                                  ab5[b2][:, :], qn5[b2][:, :], ctab[m % 2][:, :]),
                                  waits=[(tsem, ct)])
                              vr = P.op("vector", lambda h, b2=b2, m=m: h.tensor_mul(
                                  bb5[b2][:, :], qn5[b2][:, :], stab[m % 2][:, :]), inc="V")
                              v_tab[m] = vr
                              spg = [(f"dQ{b2}", sp_q[b2])] if b2 in sp_q else []
                              P.op("vector", lambda h, b2=b2: h.tensor_add(
                                  qrb[b2][:, :].rearrange("p (g d) -> p g d", d=64)[:, :, 0:32],
                                  ab5[b2][:, :].rearrange("p (g d) -> p g d", d=64)[:, :, 0:32],
                                  bb5[b2][:, :].rearrange("p (g d) -> p g d", d=64)[:, :, 32:64]),
                                  waits=spg)
                              vr2 = P.op("vector", lambda h, b2=b2: h.tensor_sub(
                                  qrb[b2][:, :].rearrange("p (g d) -> p g d", d=64)[:, :, 32:64],
                                  ab5[b2][:, :].rearrange("p (g d) -> p g d", d=64)[:, :, 32:64],
                                  bb5[b2][:, :].rearrange("p (g d) -> p g d", d=64)[:, :, 0:32]),
                                  inc="V")
                              sp_q[b2] = P.op("sync", lambda h, b2=b2, m=m, n=n, dsp=dsp: h.dma_start(
                                  dsp[m * 128:(m + 1) * 128, n * 512:n * 512 + 512],
                                  qrb[b2][:, :]), waits=[("V", vr2)], inc=f"dQ{b2}", inc_n=16)
                          else:
                              spg = [(f"dQ{bi}", sp_q[bi])] if bi in sp_q else []
                              sv = P.op("scalar", lambda h, pq=pq, bi=bi: h.activation(
                                  qrb[bi][:, :], pq[:, :], AF.Copy),
                                  waits=[("T", tq)] + spg, inc="S")
                              s_post[bi] = sv
                              v_post.pop(bi, None)
                              sp_q[bi] = P.op("sync", lambda h, bi=bi, m=m, n=n, dsp=dsp: h.dma_start(
                                  dsp[m * 128:(m + 1) * 128, n * 512:n * 512 + 512],
                                  qrb[bi][:, :]), waits=[("S", sv)], inc=f"dQ{bi}", inc_n=16)

              with contextlib.ExitStack() as pq_scope:
                  QE = pq_scope.enter_context
                  xnT_o = [QE(nc.sbuf_tensor(f"xnT_o{k}", [128, TOK], BF16,
                                             is_transpose_dest=True)) for k in range(8)]
                  wqb = QE(nc.sbuf_tensor("wqb", [128, 8 * C], BF16))
                  gateb = QE(nc.sbuf_tensor("gateb", [128, 64], BF16))
                  lqkb = QE(nc.sbuf_tensor("lqkb", [128, 4 * HD], F32))
                  gps = [QE(nc.psum_tensor(f"gps{j}", [128, 512], F32)) for j in range(2)]
                  rt = QE(nc.sbuf_tensor("rt", [128, 8], F32))
                  r8 = QE(nc.sbuf_tensor("r8", [128, 72], F32))
                  lt = QE(nc.sbuf_tensor("lt", [128, HD], F32))
                  lam_c = QE(nc.sbuf_tensor("lam_c", [128, 2], F32))

                  tl_o = None
                  for k in range(8):
                      tl_o = P.op("sync", lambda h, k=k: h.dma_start_transpose(
                          xnT_o[k][:, :], d_xn_o[:, k * 128:(k + 1) * 128]),
                          inc="dT", inc_n=16)
                  g_wq = None
                  for k in range(8):
                      g_wq = P.op("gpsimd", lambda h, k=k: h.dma_start(
                          wqb[:, k * C:(k + 1) * C], d_wq[k * 128:(k + 1) * 128, :]),
                          inc="dG", inc_n=16)
                  g_gate = P.op("gpsimd", lambda h: h.dma_start(
                      gateb[:, :].rearrange("p (k e) -> p k e", e=NEXP),
                      d_gate.rearrange("(k p) e -> p k e", p=128)), inc="dG", inc_n=16)
                  g_lqk = P.op("gpsimd", lambda h: h.dma_start(
                      lqkb[:, :], d_lqk.broadcast_to([128, 4 * HD])), inc="dG", inc_n=16)
                  g_wq = g_gate = g_lqk

                  P.op("vector", lambda h: h.tensor_mul(
                      lt[:, :], lqkb[:, 0:64], lqkb[:, 64:128]), waits=[("dG", g_lqk)])
                  P.op("vector", lambda h: h.reduce_sum(lam_c[:, 0:1], lt[:, :], axis=AX.X))
                  P.op("vector", lambda h: h.tensor_mul(
                      lt[:, :], lqkb[:, 128:192], lqkb[:, 192:256]))
                  vl = P.op("vector", lambda h: h.reduce_sum(
                      lam_c[:, 1:2], lt[:, :], axis=AX.X), inc="V")
                  P.op("scalar", lambda h: h.activation(
                      lam_c[:, 0:1], lam_c[:, 0:1], AF.Exp), waits=[("V", vl)], inc="S")
                  sl = P.op("scalar", lambda h: h.activation(
                      lam_c[:, 1:2], lam_c[:, 1:2], AF.Exp), inc="S")
                  P.op("vector", lambda h: h.tensor_sub(
                      lamneg[:, 0:1], lam_c[:, 0:1], lam_c[:, 1:2]), waits=[("S", sl)])
                  P.op("vector", lambda h: h.tensor_scalar(
                      lamneg[:, 0:1], lamneg[:, 0:1], LAM_INIT, -1.0, ALU.add, ALU.mult))

                  v_wtok, s_e2g = {}, {}
                  for m in range(16):
                      pg = gps[m % 2]
                      guard = []
                      if m - 2 in v_wtok:
                          guard = [("V", v_wtok[m - 2]), ("S", s_e2g[m - 2])]
                      tg = None
                      for k in range(8):
                          r = P.op("tensor", lambda h, k=k, m=m, pg=pg: h.matmul(
                              pg[:, 0:8], xnT_o[k][:, m * 128:(m + 1) * 128],
                              gateb[:, k * 8:(k + 1) * 8], start=(k == 0), stop=(k == 7)),
                              waits=([("dT", tl_o), ("dG", g_gate)] + guard) if k == 0 else [],
                              inc="T" if k == 7 else None)
                          if r is not None:
                              tg = r
                      ag = [("V", v_wtok[m - 1])] if m - 1 in v_wtok else []
                      P.op("vector", lambda h, pg=pg: h.reduce_max(
                          rt[:, 0:1], pg[:, 0:8], axis=AX.X), waits=[("T", tg)])
                      P.op("vector", lambda h, pg=pg: h.tensor_scalar(
                          r8[:, 0:8], pg[:, 0:8], rt[:, 0:1], None, ALU.is_equal))
                      sab = P.op("scalar", lambda h, pg=pg: h.activation(
                          r8[:, 8:16], pg[:, 0:8], AF.Abs), waits=[("T", tg)] + ag, inc="S")
                      P.op("vector", lambda h: h.tensor_scalar(
                          r8[:, 16:24], r8[:, 8:16], rt[:, 0:1], None, ALU.max),
                          waits=[("S", sab)])
                      P.op("vector", lambda h, pg=pg: h.tensor_scalar(
                          r8[:, 24:32], pg[:, 0:8], rt[:, 0:1], None, ALU.subtract))
                      P.op("vector", lambda h: h.scalar_tensor_tensor(
                          r8[:, 32:40], r8[:, 16:24], JIT2, r8[:, 24:32], ALU.mult, ALU.add))
                      P.op("vector", lambda h: h.tensor_scalar(
                          r8[:, 32:40], r8[:, 32:40], 0.0, None, ALU.is_ge))
                      vm = P.op("vector", lambda h: h.tensor_scalar(
                          rt[:, 1:2], rt[:, 0:1], -1.0, None, ALU.mult), inc="V")
                      se = P.op("scalar", lambda h, pg=pg: h.activation(
                          r8[:, 40:48], pg[:, 0:8], AF.Exp, bias=rt[:, 1:2]),
                          waits=[("V", vm)], inc="S")
                      P.op("vector", lambda h: h.tensor_mul(
                          r8[:, 40:48], r8[:, 40:48], r8[:, 32:40]), waits=[("S", se)])
                      P.op("vector", lambda h: h.reduce_sum(rt[:, 4:5], r8[:, 40:48], axis=AX.X))
                      P.op("vector", lambda h: h.reciprocal(rt[:, 6:7], rt[:, 4:5]))
                      P.op("vector", lambda h, pg=pg: h.scalar_tensor_tensor(
                          r8[:, 56:64], r8[:, 0:8], NEG_BIG, pg[:, 0:8], ALU.mult, ALU.add))
                      P.op("vector", lambda h: h.reduce_max(rt[:, 2:3], r8[:, 56:64], axis=AX.X))
                      P.op("vector", lambda h: h.tensor_scalar(
                          r8[:, 48:56], r8[:, 56:64], rt[:, 2:3], None, ALU.is_equal))
                      P.op("vector", lambda h: h.tensor_scalar(
                          r8[:, 16:24], r8[:, 8:16], rt[:, 2:3], None, ALU.max))
                      P.op("vector", lambda h, pg=pg: h.tensor_scalar(
                          r8[:, 24:32], pg[:, 0:8], rt[:, 2:3], None, ALU.subtract))
                      P.op("vector", lambda h: h.scalar_tensor_tensor(
                          r8[:, 24:32], r8[:, 16:24], JIT2, r8[:, 24:32], ALU.mult, ALU.add))
                      P.op("vector", lambda h: h.tensor_scalar(
                          r8[:, 24:32], r8[:, 24:32], 0.0, None, ALU.is_ge))
                      P.op("vector", lambda h: h.tensor_sub(
                          r8[:, 24:32], r8[:, 24:32], r8[:, 0:8]))
                      vm2 = P.op("vector", lambda h: h.tensor_scalar(
                          rt[:, 3:4], rt[:, 2:3], -1.0, None, ALU.mult), inc="V")
                      se2 = P.op("scalar", lambda h, pg=pg: h.activation(
                          r8[:, 64:72], pg[:, 0:8], AF.Exp, bias=rt[:, 3:4]),
                          waits=[("V", vm2)], inc="S")
                      s_e2g[m] = se2
                      P.op("vector", lambda h: h.tensor_mul(
                          r8[:, 64:72], r8[:, 64:72], r8[:, 24:32]), waits=[("S", se2)])
                      P.op("vector", lambda h: h.reduce_sum(rt[:, 5:6], r8[:, 64:72], axis=AX.X))
                      P.op("vector", lambda h: h.reciprocal(rt[:, 7:8], rt[:, 5:6]))
                      P.op("vector", lambda h: h.tensor_scalar(
                          r8[:, 48:56], r8[:, 48:56], rt[:, 7:8], None, ALU.mult))
                      v_wtok[m] = P.op("vector", lambda h, m=m: h.scalar_tensor_tensor(
                          wtok_sb[:, m * 8:(m + 1) * 8], r8[:, 0:8], rt[:, 6:7],
                          r8[:, 48:56], ALU.mult, ALU.add), inc="V")

                  qkv_part("q", xnT_o, 16, wqb, g_wq, tl_o, d_q_s, d_cos_o, d_sin_o, True, True)
                  P.barrier()
                  P.flush()

              with contextlib.ExitStack() as pk_scope:
                  KE = pk_scope.enter_context
                  xnT_c = [KE(nc.sbuf_tensor(f"xnT_c{k}", [128, T], BF16,
                                             is_transpose_dest=True)) for k in range(8)]
                  wkb = KE(nc.sbuf_tensor("wkb", [128, 8 * C], BF16))
                  wvb = KE(nc.sbuf_tensor("wvb", [128, 8 * C], BF16))
                  tl_c = None
                  for k in range(8):
                      tl_c = P.op("sync", lambda h, k=k: h.dma_start_transpose(
                          xnT_c[k][:, :], d_xn_c[:, k * 128:(k + 1) * 128]),
                          inc="dT", inc_n=16)
                  g_wk = None
                  for k in range(8):
                      g_wk = P.op("gpsimd", lambda h, k=k: h.dma_start(
                          wkb[:, k * C:(k + 1) * C], d_wk[k * 128:(k + 1) * 128, :]),
                          inc="dG", inc_n=16)
                  g_wv = None
                  for k in range(8):
                      g_wv = P.op("gpsimd", lambda h, k=k: h.dma_start(
                          wvb[:, k * C:(k + 1) * C], d_wv[k * 128:(k + 1) * 128, :]),
                          inc="dG", inc_n=16)
                  g_wk = g_wv
                  qkv_part("k", xnT_c, 32, wkb, g_wk, tl_c, d_k_s, d_cos_c, d_sin_c, True, False)
                  qkv_part("v", xnT_c, 32, wvb, g_wv, tl_c, d_v_s, None, None, False, False)
                  P.barrier()
                  P.flush()

        with contextlib.ExitStack() as ph:
          if phases >= 3:
              PE = ph.enter_context
              xnT_o = [PE(nc.sbuf_tensor(f"f_xnT{k}", [128, TOK], BF16,
                                         is_transpose_dest=True)) for k in range(8)]
              w1c = [PE(nc.sbuf_tensor(f"w1c{j}", [128, 8192], BF16)) for j in range(2)]
              w3c = [PE(nc.sbuf_tensor(f"w3c{j}", [128, 8192], BF16)) for j in range(2)]
              w2c = [PE(nc.sbuf_tensor(f"w2c{j}", [128, 8192], BF16)) for j in range(2)]
              sbuf_s = [PE(nc.sbuf_tensor(f"f_s{j}", [128, TBS], BF16)) for j in range(2)]
              hbuf = [PE(nc.sbuf_tensor(f"f_h{j}", [128, TBS], BF16)) for j in range(2)]
              ups = [PE(nc.psum_tensor(f"ups{j}", [128, 512], F32)) for j in range(2)]
              wps = [PE(nc.psum_tensor(f"wps{j}", [128, 512], F32)) for j in range(2)]
              yps = PE(nc.psum_tensor("yps", [128, 2048], F32))

              tl = None
              for k in range(8):
                  tl = P.op("sync", lambda h, k=k: h.dma_start_transpose(
                      xnT_o[k][:, :], d_xn_o[:, k * 128:(k + 1) * 128]), inc="dT", inc_n=16)
              P.op("vector", lambda h: h.memset(y_sb[:, :], 0.0), inc="V")

              groups = [(e, fg) for e in range(NEXP) for fg in range(NGRP)]
              pe_end, dg_end, v_ystt, t_mm2 = {}, {}, {}, {}
              u_guard, w_guard, s_guard, h_guard = {}, {}, {}, {}
              for g, (e, fg) in enumerate(groups):
                  buf = g % 2
                  gg = [("T", pe_end[g - 2])] if g - 2 in pe_end else []
                  gsem = f"dG{g % 2}"
                  for k in range(8):
                      P.op("gpsimd", lambda h, e=e, fg=fg, k=k, buf=buf: h.dma_start(
                          w1c[buf][:, k * 1024:(k + 1) * 1024],
                          d_w1[e, k * 128:(k + 1) * 128, fg * FG:(fg + 1) * FG]),
                          waits=gg if k == 0 else [], inc=gsem, inc_n=16)
                  for k in range(8):
                      P.op("gpsimd", lambda h, e=e, fg=fg, k=k, buf=buf: h.dma_start(
                          w3c[buf][:, k * 1024:(k + 1) * 1024],
                          d_w3[e, k * 128:(k + 1) * 128, fg * FG:(fg + 1) * FG]),
                          inc=gsem, inc_n=16)
                  for ft in range(8):
                      dg_end[g] = P.op("gpsimd", lambda h, e=e, fg=fg, ft=ft, buf=buf: h.dma_start(
                          w2c[buf][:, ft * 1024:(ft + 1) * 1024],
                          d_w2[e, fg * FG + ft * 128: fg * FG + (ft + 1) * 128, :]),
                          inc=gsem, inc_n=16)

                  for tb in range(NTB):
                      t_mm1, t_mm3, s_silu, v_h = {}, {}, {}, {}

                      def emit_mm2(fm, g=g, tb=tb, buf=buf, v_h=None):
                          sl = fm % 2
                          w = [("V", v_h[fm])]
                          if fm == 0:
                              if (g, tb - 1) in v_ystt:
                                  w.append(("V", v_ystt[(g, tb - 1)]))
                              elif (g - 1, NTB - 1) in v_ystt:
                                  w.append(("V", v_ystt[(g - 1, NTB - 1)]))
                          last = None
                          for half in range(2):
                              for n2 in range(2):
                                  last = P.op("tensor",
                                              lambda h, sl=sl, half=half, n2=n2, fm=fm, buf=buf: h.matmul(
                                      yps[:, half * 1024 + n2 * 512: half * 1024 + n2 * 512 + 512],
                                      hbuf[sl][:, half * 128:(half + 1) * 128],
                                      w2c[buf][:, fm * 1024 + n2 * 512: fm * 1024 + n2 * 512 + 512],
                                      start=(fm == 0), stop=(fm == 7)),
                                      waits=w if (half == 0 and n2 == 0) else [],
                                      inc="T")
                          t_mm2[(g, tb, fm)] = last
                          h_guard[sl] = ("T", last)

                      for fm in range(8):
                          sl = fm % 2
                          w1l = [(f"dG{g % 2}", dg_end[g])] if (tb == 0 and fm == 0) else []
                          if sl in u_guard:
                              w1l.append(u_guard[sl])
                          if tb == 0 and fm == 0 and g == 0:
                              w1l.append(("dT", tl))
                          tm1 = None
                          for k in range(8):
                              r = P.op("tensor",
                                       lambda h, k=k, fm=fm, tb=tb, buf=buf, sl=sl: h.matmul(
                                  ups[sl][:, 0:TBS],
                                  w1c[buf][:, k * 1024 + fm * 128: k * 1024 + (fm + 1) * 128],
                                  xnT_o[k][:, tb * TBS:(tb + 1) * TBS],
                                  start=(k == 0), stop=(k == 7)),
                                  waits=w1l if k == 0 else [], inc="T" if k == 7 else None)
                              if r is not None:
                                  tm1 = r
                          t_mm1[fm] = tm1
                          w3l = [w_guard[sl]] if sl in w_guard else []
                          tm3 = None
                          for k in range(8):
                              r = P.op("tensor",
                                       lambda h, k=k, fm=fm, tb=tb, buf=buf, sl=sl: h.matmul(
                                  wps[sl][:, 0:TBS],
                                  w3c[buf][:, k * 1024 + fm * 128: k * 1024 + (fm + 1) * 128],
                                  xnT_o[k][:, tb * TBS:(tb + 1) * TBS],
                                  start=(k == 0), stop=(k == 7)),
                                  waits=w3l if k == 0 else [], inc="T" if k == 7 else None)
                              if r is not None:
                                  tm3 = r
                          t_mm3[fm] = tm3
                          if fm > 0:
                              emit_mm2(fm - 1, v_h=v_h)
                          sg = [s_guard[sl]] if sl in s_guard else []
                          s_silu[fm] = P.op("scalar", lambda h, sl=sl: h.activation(
                              sbuf_s[sl][:, :], ups[sl][:, 0:TBS], AF.Silu),
                              waits=[("T", t_mm1[fm])] + sg, inc="S")
                          u_guard[sl] = ("S", s_silu[fm])
                          hg = [h_guard[sl]] if sl in h_guard else []
                          v_h[fm] = P.op("vector", lambda h, sl=sl: h.tensor_mul(
                              hbuf[sl][:, :], sbuf_s[sl][:, :], wps[sl][:, 0:TBS]),
                              waits=[("S", s_silu[fm]), ("T", t_mm3[fm])] + hg, inc="V")
                          w_guard[sl] = ("V", v_h[fm])
                          s_guard[sl] = ("V", v_h[fm])
                      emit_mm2(7, v_h=v_h)
                      tile0 = tb * 2
                      for half in range(2):
                          v_ystt[(g, tb)] = P.op("vector",
                                                 lambda h, half=half, tile0=tile0, e=e: h.scalar_tensor_tensor(
                              y_sb[:, (tile0 + half) * 1024:(tile0 + half + 1) * 1024],
                              yps[:, half * 1024:(half + 1) * 1024],
                              wtok_sb[:, (tile0 + half) * 8 + e: (tile0 + half) * 8 + e + 1],
                              y_sb[:, (tile0 + half) * 1024:(tile0 + half + 1) * 1024],
                              ALU.mult, ALU.add),
                              waits=[("T", t_mm2[(g, tb, 7)])], inc="V")
                  pe_end[g] = t_mm2[(g, NTB - 1, 7)]
              P.barrier()
              P.flush()

        with contextlib.ExitStack() as ph:
          if phases >= 4:
              PE = ph.enter_context
              qt = [PE(nc.sbuf_tensor(f"a_qt{j}", [128, TOK], BF16, is_transpose_dest=True))
                    for j in range(2)]
              kt = [PE(nc.sbuf_tensor(f"a_kt{j}", [128, T], BF16, is_transpose_dest=True))
                    for j in range(2)]
              vh = [PE(nc.sbuf_tensor(f"a_vh{j}", [128, T], BF16)) for j in range(2)]
              e0b = [PE(nc.sbuf_tensor(f"a_e0{j}", [128, T], BF16)) for j in range(2)]
              e1b = [PE(nc.sbuf_tensor(f"a_e1{j}", [128, T], BF16)) for j in range(2)]
              ptrb = PE(nc.sbuf_tensor("a_ptr", [128, 8192], BF16))
              eraw = [PE(nc.sbuf_tensor(f"a_eraw{j}", [128, 512], BF16)) for j in range(2)]
              maskb = PE(nc.sbuf_tensor("a_mask", [128, 1024], BF16))
              zc = PE(nc.sbuf_tensor("a_zc", [128, 48], F32))
              avsq = PE(nc.sbuf_tensor("a_avsq", [128, 128], F32))
              avsb = [PE(nc.sbuf_tensor(f"a_avsb{j}", [128, 128], F32)) for j in range(2)]
              avnb = [PE(nc.sbuf_tensor(f"a_avn{j}", [128, 128], BF16)) for j in range(2)]
              s0ps = PE(nc.psum_tensor("s0ps", [128, 1024], F32))
              s1ps = PE(nc.psum_tensor("s1ps", [128, 1024], F32))
              ptrps = [PE(nc.psum_tensor(f"ptrps{j}", [128, 1024], BF16)) for j in range(2)]
              avps = [PE(nc.psum_tensor(f"avps{j}", [128, 512], F32)) for j in range(2)]

              gm = P.op("gpsimd", lambda h: h.dma_start(maskb[:, :], d_mask),
                        inc="dG", inc_n=16)
              P.op("vector", lambda h: h.memset(zc[:, :], 0.0), inc="V")
              t_avend = {}
              sps_guard = {}
              av_guard = {}
              avsb_g = {}
              avn_dg = {}

              def emit_loads(hp, ctx):
                  hb = hp % 2
                  lg = [("T", t_avend[hp - 2])] if hp - 2 in t_avend else []
                  asem = f"dA{hp % 2}"
                  P.op("sync", lambda h, hp=hp, hb=hb: h.dma_start_transpose(
                      qt[hb][:, :], d_q_s[:, hp * 128:(hp + 1) * 128]), waits=lg,
                      inc=asem, inc_n=16)
                  P.op("sync", lambda h, hp=hp, hb=hb: h.dma_start_transpose(
                      kt[hb][:, :], d_k_s[:, hp * 128:(hp + 1) * 128]), inc=asem, inc_n=16)
                  ld_h = None
                  for kb4 in range(8):
                      ld_h = P.op("sync", lambda h, hp=hp, hb=hb, kb4=kb4: h.dma_start(
                          vh[hb][:, kb4 * 512:(kb4 + 1) * 512].rearrange(
                              "p (kb d) -> p kb d", d=128),
                          d_v_s[kb4 * 512:(kb4 + 1) * 512,
                                hp * 128:(hp + 1) * 128].rearrange(
                              "(kb p) d -> p kb d", p=128)),
                          inc=asem, inc_n=16)
                  ctx[("ld", hp)] = ld_h

              def front_units(hp, i, ctx):
                  hb = hp % 2
                  asem = f"dA{hp % 2}"
                  kcn = i // 2 + 1
                  eslot = i % 2
                  zb = 24 * eslot
                  s_e, v_e = {}, {}
                  nfull = kcn - 1
                  npairs = (nfull + 1) // 2
                  units = []

                  def mk_pair(j):
                   def _u():
                      k0 = 2 * j
                      wide = min(2, nfull - k0)
                      gq = [(asem, ctx[("ld", hp)])] if (i <= 1 and j == 0) else []
                      gps_ = [sps_guard[0]] if 0 in sps_guard else []
                      eg = []
                      if j == 0 and ("tr_done", hp, i - 2) in ctx:
                          eg = [("T", ctx[("tr_done", hp, i - 2)])]
                      elif j == 0 and i < 2 and ("tr_last", hp - 1) in ctx:
                          eg = [("T", ctx[("tr_last", hp - 1)])]
                      t_s0 = t_s1 = None
                      for c2_ in range(wide):
                          kc = k0 + c2_
                          t_s0 = P.op("tensor", lambda h, hb=hb, i=i, kc=kc, c2_=c2_: h.matmul(
                              s0ps[:, c2_ * 512:(c2_ + 1) * 512],
                              qt[hb][0:64, i * 128:(i + 1) * 128],
                              kt[hb][0:64, kc * 512:(kc + 1) * 512], start=True, stop=True),
                              waits=(gq + gps_ + eg) if c2_ == 0 else [], inc="T")
                          t_s1 = P.op("tensor", lambda h, hb=hb, i=i, kc=kc, c2_=c2_: h.matmul(
                              s1ps[:, c2_ * 512:(c2_ + 1) * 512],
                              qt[hb][64:128, i * 128:(i + 1) * 128],
                              kt[hb][64:128, kc * 512:(kc + 1) * 512], start=True, stop=True),
                              inc="T")
                      s_e[(0, j)] = P.op("scalar", lambda h, k0=k0, wide=wide, eslot=eslot, zb=zb: h.activation(
                          e0b[eslot][:, k0 * 512:(k0 + wide) * 512],
                          s0ps[:, 0:wide * 512], AF.Exp,
                          accum_out=zc[:, zb + k0 // 2:zb + k0 // 2 + 1]),
                          waits=[("T", t_s0)], inc="S")
                      s_e[(1, j)] = P.op("scalar", lambda h, k0=k0, wide=wide, eslot=eslot, zb=zb: h.activation(
                          e1b[eslot][:, k0 * 512:(k0 + wide) * 512],
                          s1ps[:, 0:wide * 512], AF.Exp,
                          accum_out=zc[:, zb + 8 + k0 // 2:zb + 9 + k0 // 2]),
                          waits=[("T", t_s1)], inc="S")
                      sps_guard[0] = ("S", s_e[(1, j)])
                   return _u

                  for j in range(npairs):
                      units.append(mk_pair(j))

                  def diag_unit():
                      kc = kcn - 1
                      par = i % 2
                      gq = [(asem, ctx[("ld", hp)])] if (i <= 1 and kc == 0) else []
                      gps_ = [sps_guard[0]] if 0 in sps_guard else []
                      eg = []
                      if kc == 0 and ("tr_done", hp, i - 2) in ctx:
                          eg = [("T", ctx[("tr_done", hp, i - 2)])]
                      elif kc == 0 and i < 2 and ("tr_last", hp - 1) in ctx:
                          eg = [("T", ctx[("tr_last", hp - 1)])]
                      t_s0 = P.op("tensor", lambda h, hb=hb, i=i, kc=kc: h.matmul(
                          s0ps[:, 0:512], qt[hb][0:64, i * 128:(i + 1) * 128],
                          kt[hb][0:64, kc * 512:(kc + 1) * 512], start=True, stop=True),
                          waits=gq + gps_ + eg, inc="T")
                      t_s1 = P.op("tensor", lambda h, hb=hb, i=i, kc=kc: h.matmul(
                          s1ps[:, 512:1024], qt[hb][64:128, i * 128:(i + 1) * 128],
                          kt[hb][64:128, kc * 512:(kc + 1) * 512], start=True, stop=True),
                          inc="T")
                      erg = []
                      if ("v_eraw", hp, i - 2) in ctx:
                          erg = [("V", ctx[("v_eraw", hp, i - 2)])]
                      elif i < 2 and ("v_eraw_last", hp - 1) in ctx:
                          erg = [("V", ctx[("v_eraw_last", hp - 1)])]
                      sr0 = P.op("scalar", lambda h, eslot=eslot: h.activation(
                          eraw[eslot][:, :], s0ps[:, 0:512], AF.Exp),
                          waits=[("T", t_s0)] + erg, inc="S")
                      v_e[(0, kc)] = P.op("vector", lambda h, kc=kc, par=par, eslot=eslot, zb=zb, npairs=npairs: h.scalar_tensor_tensor(
                          e0b[eslot][:, kc * 512:(kc + 1) * 512], eraw[eslot][:, :], 1.0,
                          maskb[:, par * 512:(par + 1) * 512], ALU.mult, ALU.mult,
                          accum_out=zc[:, zb + npairs:zb + npairs + 1]),
                          waits=[("S", sr0), ("dG", gm)], inc="V")
                      sr1 = P.op("scalar", lambda h, eslot=eslot: h.activation(
                          eraw[eslot][:, :], s1ps[:, 512:1024], AF.Exp),
                          waits=[("T", t_s1), ("V", v_e[(0, kc)])], inc="S")
                      v_e[(1, kc)] = P.op("vector", lambda h, kc=kc, par=par, eslot=eslot, zb=zb, npairs=npairs: h.scalar_tensor_tensor(
                          e1b[eslot][:, kc * 512:(kc + 1) * 512], eraw[eslot][:, :], 1.0,
                          maskb[:, par * 512:(par + 1) * 512], ALU.mult, ALU.mult,
                          accum_out=zc[:, zb + 8 + npairs:zb + 9 + npairs]),
                          waits=[("S", sr1)], inc="V")
                      ctx[("v_eraw", hp, i)] = v_e[(1, kc)]
                      ctx[("v_eraw_last", hp)] = v_e[(1, kc)]
                      sps_guard[0] = ("S", sr1)

                  units.append(diag_unit)

                  def z_unit():
                      kcn_ = i // 2 + 1
                      wz = [("S", max(s_e.values()))] if s_e else []
                      if v_e:
                          wz.append(("V", v_e[(1, kcn_ - 1)]))
                      ctx[("exps_w", hp, i)] = list(wz)
                      P.op("vector", lambda h, zb=zb, npairs=npairs: h.reduce_sum(
                          zc[:, zb + 16:zb + 17], zc[:, zb:zb + npairs + 1], axis=AX.X), waits=wz)
                      P.op("vector", lambda h, zb=zb: h.reciprocal(
                          zc[:, zb + 16:zb + 17], zc[:, zb + 16:zb + 17]))
                      P.op("vector", lambda h, zb=zb, npairs=npairs: h.reduce_sum(
                          zc[:, zb + 17:zb + 18], zc[:, zb + 8:zb + 9 + npairs], axis=AX.X))
                      P.op("vector", lambda h, zb=zb: h.reciprocal(
                          zc[:, zb + 17:zb + 18], zc[:, zb + 17:zb + 18]))
                      vz = P.op("vector", lambda h, zb=zb: h.tensor_mul(
                          zc[:, zb + 17:zb + 18], zc[:, zb + 17:zb + 18], lamneg[:, 0:1]), inc="V")
                      ctx[("v_z", hp, i)] = vz

                  units.append(z_unit)
                  return units

              def back_units(hp, i, ctx):
                  hb = hp % 2
                  kcn = i // 2 + 1
                  ext = 512 * kcn
                  nkb = ext // 128
                  eslot = i % 2
                  ngr = (nkb + 7) // 8
                  v_evac = {}
                  t_av = {}
                  groups = [(side, gr) for side in range(2) for gr in range(ngr)]
                  st = {"t_tr_last": None}
                  units = []

                  def mk_tr(gidx, side, gr):
                      def _u():
                          eb = e0b[eslot] if side == 0 else e1b[eslot]
                          kbs = list(range(gr * 8, min((gr + 1) * 8, nkb)))
                          pg_ = [("V", v_evac[gidx - 2])] if gidx - 2 in v_evac else []
                          aw = [("T", ctx[("t_avslot", gidx)])] if ("t_avslot", gidx) in ctx else []
                          exps_done = ctx[("exps_w", hp, i)]
                          t_tr = None
                          for kb in kbs:
                              r = P.op("tensor", lambda h, kb=kb, gidx=gidx, eb=eb: h.transpose(
                                  ptrps[gidx % 2][:, (kb % 8) * 128:((kb % 8) + 1) * 128],
                                  eb[:, kb * 128:(kb + 1) * 128], identb[:, :]),
                                  waits=(exps_done + pg_ + aw + [("dG", g_id)]) if kb == kbs[0] else [],
                                  inc="T" if kb == kbs[-1] else None)
                              if r is not None:
                                  t_tr = r
                          st["t_tr_last"] = t_tr
                          wdt = len(kbs) * 128
                          v_evac[gidx] = P.op("vector", lambda h, gidx=gidx, wdt=wdt: h.tensor_copy(
                              ptrb[:, gidx * 1024: gidx * 1024 + wdt],
                              ptrps[gidx % 2][:, 0:wdt]), waits=[("T", t_tr)], inc="V")
                          ctx[("tr_done", hp, i)] = t_tr
                          ctx[("tr_last", hp)] = t_tr
                      return _u

                  def mk_av(gidx, side, gr):
                      def _u():
                          kbs = list(range(gr * 8, min((gr + 1) * 8, nkb)))
                          ag = av_guard.get(side)
                          last = None
                          for kb in kbs:
                              last = P.op("tensor", lambda h, kb=kb, hb=hb, side=side, gidx=gidx, nkb=nkb: h.matmul(
                                  avps[side][:, 0:128],
                                  ptrb[:, gidx * 1024 + (kb % 8) * 128: gidx * 1024 + (kb % 8 + 1) * 128],
                                  vh[hb][:, kb * 128:(kb + 1) * 128],
                                  start=(kb == 0), stop=(kb == nkb - 1)),
                                  waits=([("V", v_evac[gidx])] + ([ag] if (ag and kb == 0) else [])) if kb == kbs[0] else [],
                                  inc="T" if kb == kbs[-1] else None)
                          ctx[("t_avslot", gidx)] = last
                          t_av[side] = last
                          ctx[("t_av", hp, i)] = (t_av.get(0), t_av.get(1))
                      return _u

                  tr_units = [mk_tr(g, sd, gr) for g, (sd, gr) in enumerate(groups)]
                  av_units = [mk_av(g, sd, gr) for g, (sd, gr) in enumerate(groups)]
                  return tr_units, av_units

              def emit_back_post(hp, i, ctx):
                  ab_ = i % 2
                  zb = 24 * ab_
                  ta0, ta1 = ctx[("t_av", hp, i)]
                  ag2 = [avsb_g[ab_]] if ab_ in avsb_g else []
                  c1 = P.op("vector", lambda h, ab_=ab_, zb=zb: h.tensor_scalar(
                      avsb[ab_][:, :], avps[0][:, 0:128], zc[:, zb + 16:zb + 17],
                      None, ALU.mult), waits=[("T", ta0)] + ag2, inc="V")
                  c2 = P.op("vector", lambda h, ab_=ab_, zb=zb: h.scalar_tensor_tensor(
                      avsb[ab_][:, :], avps[1][:, 0:128], zc[:, zb + 17:zb + 18],
                      avsb[ab_][:, :], ALU.mult, ALU.add), waits=[("T", ta1)], inc="V")
                  av_guard[0] = ("V", c1)
                  av_guard[1] = ("V", c2)
                  P.op("vector", lambda h, ab_=ab_: h.reduce_sum(
                      zc[:, 44:45], avsb[ab_][:, :], axis=AX.X))
                  sq_ = P.op("scalar", lambda h, ab_=ab_: h.activation(
                      avsq[:, :], avsb[ab_][:, :], AF.Square,
                      accum_out=zc[:, 45:46]), waits=[("V", c2)], inc="S")
                  P.op("vector", lambda h: h.tensor_scalar(
                      zc[:, 44:45], zc[:, 44:45], 1.0 / 128, None, ALU.mult))
                  P.op("vector", lambda h: h.tensor_scalar(
                      zc[:, 45:46], zc[:, 45:46], 1.0 / 128, None, ALU.mult),
                      waits=[("S", sq_)])
                  P.op("vector", lambda h: h.tensor_scalar(
                      zc[:, 46:47], zc[:, 44:45], zc[:, 44:45], None, ALU.mult))
                  vv_ = P.op("vector", lambda h: h.tensor_sub(
                      zc[:, 45:46], zc[:, 45:46], zc[:, 46:47]), inc="V")
                  P.op("scalar", lambda h: h.activation(
                      zc[:, 45:46], zc[:, 45:46], AF.Ln, bias=constb[:, 0:1]),
                      waits=[("V", vv_)], inc="S")
                  sd_ = P.op("scalar", lambda h: h.activation(
                      zc[:, 45:46], zc[:, 45:46], AF.Exp, scale=constb[:, 3:4]), inc="S")
                  P.op("vector", lambda h: h.tensor_scalar(
                      zc[:, 45:46], zc[:, 45:46], 1.0 - LAM_INIT, None, ALU.mult),
                      waits=[("S", sd_)])
                  P.op("vector", lambda h: h.tensor_scalar(
                      zc[:, 46:47], zc[:, 45:46], -1.0, None, ALU.mult))
                  P.op("vector", lambda h: h.tensor_scalar(
                      zc[:, 46:47], zc[:, 44:45], zc[:, 46:47], None, ALU.mult))
                  ag3 = [(f"dN{i % 2}", avn_dg[i % 2])] if i % 2 in avn_dg else []
                  s_avn = P.op("vector", lambda h, ab_=ab_, i=i: h.tensor_scalar(
                      avnb[i % 2][:, :], avsb[ab_][:, :], zc[:, 45:46],
                      zc[:, 46:47], ALU.mult, ALU.add),
                      waits=ag3, inc="V")
                  avsb_g[ab_] = ("V", s_avn)
                  avn_dg[i % 2] = P.op("sync", lambda h, i=i, hp=hp: h.dma_start(
                      d_avn_s[i * 128:(i + 1) * 128, hp * 128:(hp + 1) * 128],
                      avnb[i % 2][:, :]), waits=[("V", s_avn)], inc=f"dN{i % 2}", inc_n=16)

              def interleave(a, b):
                  out = []
                  na, nb = len(a), len(b)
                  ia = ib = 0
                  tot = na + nb
                  for k in range(tot):
                      if ia * (nb or 1) <= ib * (na or 1) and ia < na:
                          out.append(a[ia]); ia += 1
                      elif ib < nb:
                          out.append(b[ib]); ib += 1
                      else:
                          out.append(a[ia]); ia += 1
                  return out

              ctx = {}
              emit_loads(0, ctx)
              emit_loads(1, ctx)
              pend = {}
              for hp in range(NHP):
                  for i in range(16):
                      if i >= 2:
                          emit_back_post(hp, i - 2, ctx)
                      fu = front_units(hp, i, ctx)
                      if i >= 1:
                          tr_u, av_u = back_units(hp, i - 1, ctx)
                          for u in interleave(fu, tr_u + av_u):
                              u()
                      else:
                          for u in fu:
                              u()
                  emit_back_post(hp, 14, ctx)
                  tr_u, av_u = back_units(hp, 15, ctx)
                  for u in tr_u + av_u:
                      u()
                  t_avend[hp] = P.cnt["T"]
                  if hp + 2 < NHP:
                      emit_loads(hp + 2, ctx)
                  emit_back_post(hp, 15, ctx)
              P.barrier()
              P.flush()


        with contextlib.ExitStack() as ph:
          if phases >= 5:
              PE = ph.enter_context
              avnT = [PE(nc.sbuf_tensor(f"pr_avnT{h2}", [128, TOK], BF16,
                                        is_transpose_dest=True)) for h2 in range(8)]
              wpb = PE(nc.sbuf_tensor("wpb", [128, 8 * C], BF16))
              xb2 = [PE(nc.sbuf_tensor(f"pr_x{j}", [128, C], F32)) for j in range(2)]
              gb = [PE(nc.sbuf_tensor(f"pr_g{j}", [128, C], F32)) for j in range(2)]
              ob = [PE(nc.sbuf_tensor(f"pr_o{j}", [128, C], F32)) for j in range(2)]
              tfb = [PE(nc.sbuf_tensor(f"pr_t{j}", [128, 512], F32)) for j in range(2)]
              prps = [PE(nc.psum_tensor(f"prps{j}", [128, 512], F32)) for j in range(2)]

              tla = None
              for h2 in range(8):
                  tla = P.op("sync", lambda h, h2=h2: h.dma_start_transpose(
                      avnT[h2][:, :], d_avn_s[:, h2 * 128:(h2 + 1) * 128]),
                      inc="dT", inc_n=16)
              g_wp = None
              for k in range(8):
                  g_wp = P.op("gpsimd", lambda h, k=k: h.dma_start(
                      wpb[:, k * C:(k + 1) * C], d_wp[k * 128:(k + 1) * 128, :]),
                      inc="dG", inc_n=16)
              v_out, st_out = {}, {}
              for m in range(16):
                  xg = [(f"dO{m % 2}", st_out[m - 2])] if m - 2 in st_out else []
                  lx = P.op("sync", lambda h, m=m: h.dma_start(
                      xb2[m % 2][:, :], d_x_own[m * 128:(m + 1) * 128, :]), waits=xg,
                      inc=f"dA{m % 2}", inc_n=16)
                  gg_ = [("V", v_out[(m - 2, 1)])] if (m - 2, 1) in v_out else []
                  sg_ = P.op("scalar", lambda h, m=m: h.activation(
                      gb[m % 2][:, :], y_sb[:, m * 1024:(m + 1) * 1024], AF.Silu),
                      waits=gg_, inc="S")
                  for n in range(2):
                      pbf = (2 * m + n) % 2
                      pg_ = [("V", v_out[(m - 1, n)])] if (m - 1, n) in v_out else []
                      t_pr = None
                      for h2 in range(8):
                          r = P.op("tensor", lambda h, h2=h2, m=m, n=n, pbf=pbf: h.matmul(
                              prps[pbf][:, :], avnT[h2][:, m * 128:(m + 1) * 128],
                              wpb[:, h2 * C + n * 512: h2 * C + n * 512 + 512],
                              start=(h2 == 0), stop=(h2 == 7)),
                              waits=([("dT", tla), ("dG", g_wp)] + pg_) if h2 == 0 else [],
                              inc="T" if h2 == 7 else None)
                          if r is not None:
                              t_pr = r
                      og = [(f"dO{m % 2}", st_out[m - 2])] if (n == 0 and m - 2 in st_out) else []
                      P.op("vector", lambda h, n=n, pbf=pbf, m=m: h.tensor_mul(
                          tfb[n][:, :], prps[pbf][:, :], gb[m % 2][:, n * 512:(n + 1) * 512]),
                          waits=[("T", t_pr), ("S", sg_)])
                      v_out[(m, n)] = P.op("vector", lambda h, m=m, n=n: h.tensor_add(
                          ob[m % 2][:, n * 512:(n + 1) * 512], tfb[n][:, :],
                          xb2[m % 2][:, n * 512:(n + 1) * 512]),
                          waits=[(f"dA{m % 2}", lx)] + og, inc="V")
                  st_out[m] = P.op("sync", lambda h, m=m: h.dma_start(
                      d_out[m * 128:(m + 1) * 128, :], ob[m % 2][:, :]),
                      waits=[("V", v_out[(m, 1)])], inc=f"dO{m % 2}", inc_n=16)

              if debug:
                  vd = P.op("vector", lambda h: h.tensor_copy(
                      tfb[0][:, 0:128], wtok_sb[:, :]), inc="V")
                  P.op("sync", lambda h: h.dma_start(dbg["dbg_wtok"], tfb[0][:, 0:128]),
                       waits=[("V", vd)], inc="dT", inc_n=16)
                  P.op("sync", lambda h: h.dma_start(dbg["dbg_y"], y_sb[:, :]),
                       inc="dT", inc_n=16)
                  for name, src in [("dbg_xn_own", d_xn_o), ("dbg_xn_ctx", d_xn_c),
                                    ("dbg_q", d_q_s), ("dbg_k", d_k_s), ("dbg_v", d_v_s),
                                    ("dbg_avn", d_avn_s)]:
                      P.op("sync", lambda h, name=name, src=src: h.dma_start(
                          dbg[name], src), inc="dT", inc_n=16)
              P.barrier()
              P.flush()

    return nc



def _tables(off):
    inv_freq = (10000.0 ** (-np.arange(0, HD, 2, dtype=np.float32) / HD)).astype(np.float32)

    def tab(pos):
        fr = np.outer(pos.astype(np.float32), inv_freq)
        c = np.cos(fr).astype(np.float32)
        s = np.sin(fr).astype(np.float32)
        cd = np.concatenate([c, c], axis=1)
        sd = np.concatenate([s, s], axis=1)
        return (np.ascontiguousarray(np.tile(cd, (1, 8))),
                np.ascontiguousarray(np.tile(sd, (1, 8))))

    c_o, s_o = tab(np.arange(off, 2 * TOK, 2))
    c_c, s_c = tab(np.arange(T))
    return c_o, s_o, c_c, s_c


def _mask(off):
    r = np.arange(128)[:, None]
    j = np.arange(512)[None, :]
    m = np.zeros((128, 1024), np.float32)
    m[:, 0:512] = (j <= 2 * r + off)
    m[:, 512:1024] = (j <= 256 + 2 * r + off)
    return m.astype(ml_dtypes.bfloat16)


_NC_CACHE = {}


def kernel(L0_attn_x, L0_attn_Wq, L0_attn_Wk, L0_attn_Wv, L0_attn_Wproj,
           L0_attn_lq1, L0_attn_lk1, L0_attn_lq2, L0_attn_lk2,
           L0_ffn_gate_w, L0_ffn_w1, L0_ffn_w2, L0_ffn_w3, _debug=False):
    f32 = np.float32
    x = np.ascontiguousarray(np.asarray(L0_attn_x, dtype=f32))
    wq_t = np.ascontiguousarray(np.asarray(L0_attn_Wq, dtype=f32).T)
    wk_t = np.ascontiguousarray(np.asarray(L0_attn_Wk, dtype=f32).T)
    wv_t = np.ascontiguousarray(np.asarray(L0_attn_Wv, dtype=f32).T)
    wp_t = np.ascontiguousarray(np.asarray(L0_attn_Wproj, dtype=f32).T)
    gate_t = np.ascontiguousarray(np.asarray(L0_ffn_gate_w, dtype=f32).T)
    w1_t = np.ascontiguousarray(np.asarray(L0_ffn_w1, dtype=f32).transpose(0, 2, 1))
    w3_t = np.ascontiguousarray(np.asarray(L0_ffn_w3, dtype=f32).transpose(0, 2, 1))
    w2_t = np.ascontiguousarray(np.asarray(L0_ffn_w2, dtype=f32).transpose(0, 2, 1))
    lqk = np.ascontiguousarray(np.concatenate(
        [np.asarray(a, dtype=f32).reshape(-1) for a in
         (L0_attn_lq1, L0_attn_lk1, L0_attn_lq2, L0_attn_lk2)])[None, :])
    ident = np.eye(128, dtype=f32).astype(ml_dtypes.bfloat16)

    key = ("prog", _debug)
    if key not in _NC_CACHE:
        _NC_CACHE[key] = build_program(debug=_debug)
    nc = _NC_CACHE[key]

    tabs = {off: _tables(off) for off in (0, 1)}
    masks = {off: _mask(off) for off in (0, 1)}
    in_maps = []
    for c in range(8):
        b, off = c // 2, c % 2
        c_o, s_o, c_c, s_c = tabs[off]
        in_maps.append({
            "x_own": np.ascontiguousarray(x[b, off::2, :]),
            "x_ctx": np.ascontiguousarray(x[b]),
            "wq_t": wq_t, "wk_t": wk_t, "wv_t": wv_t, "wproj_t": wp_t,
            "gate_t": gate_t, "w1_t": w1_t, "w3_t": w3_t, "w2_t": w2_t,
            "lqk": lqk, "cos_own": c_o, "sin_own": s_o,
            "cos_ctx": c_c, "sin_ctx": s_c, "mask": masks[off], "ident": ident,
        })

    res = run_bass_kernel_spmd(nc, in_maps, core_ids=list(range(8)))
    out = np.empty((B, T, C), np.float32)
    for c in range(8):
        b, off = c // 2, c % 2
        out[b, off::2, :] = res.results[c]["out"]
    if _debug:
        return out, res
    return out
